# version 29 (final)
# baseline (speedup 1.0000x reference)
import numpy as np
import ml_dtypes
import contextlib
import concourse.bass as bass
import concourse.mybir as mybir

F32 = mybir.dt.float32
BF16 = mybir.dt.bfloat16
FP8 = mybir.dt.float8e4
I8 = mybir.dt.int8
AF = mybir.ActivationFunctionType
ALU = mybir.AluOpType
AX = mybir.AxisListType

ENGINES = ("tensor", "vector", "scalar", "gpsimd", "sync")


def host_consts():
    H, D, BS = 32, 64, 256
    base = 1.0 / 2.0 ** (8.0 / H)
    rate = base ** (np.arange(H, dtype=np.float64) + 1.0)
    factor = 1.0 - 0.0 / (32 - 1 + 1e-05) + 1e-05
    slope = (rate * factor).astype(np.float64)
    r = np.arange(BS, dtype=np.float64) + 1.0
    qd = np.exp(-slope[:, None] * r[None, :])
    kd = np.exp(-slope[:, None] * (BS - r[None, :]))
    bd = np.exp(-slope * BS)
    i = np.arange(BS)
    dd = i[None, :] - i[:, None]
    ddT = np.where(dd >= 0, np.exp(-slope[:, None, None] * dd[None]), 0.0)
    ddT_sb = np.zeros((128, 32 * 2 * 256), np.float32)
    for h in range(H):
        for jc in range(2):
            ddT_sb[:, (h * 2 + jc) * 256:(h * 2 + jc + 1) * 256] = \
                ddT[h, jc * 128:(jc + 1) * 128, :]
    ddT_sb = ddT_sb.astype(ml_dtypes.bfloat16)
    qdkd = np.zeros((128, 128), np.float32)
    for h in range(H):
        for c in range(2):
            qdkd[:, h * 2 + c] = qd[h, c * 128:(c + 1) * 128]
            qdkd[:, 64 + h * 2 + c] = kd[h, c * 128:(c + 1) * 128]
    idbf = np.eye(128).astype(ml_dtypes.bfloat16)
    idf32 = np.eye(128).astype(np.float32)

    HD, TH, L = 128, 10000.0, 2048
    inv_freq = 1.0 / TH ** (np.arange(0, HD, 2, dtype=np.float64) / HD)
    t = np.arange(L, dtype=np.float64)
    f = np.repeat(np.outer(t, inv_freq), 2, axis=-1)
    cosT = np.cos(f).T
    sinT = np.sin(f).T
    sgn = np.where(np.arange(HD) % 2 == 0, -1.0, 1.0)[:, None]
    sinS = sinT * sgn
    sc = 1.0 / np.sqrt(128.0)
    k = np.arange(128)[:, None]
    m = np.arange(1024)[None, :]
    tband = (k <= m - 384).astype(ml_dtypes.bfloat16)
    P = np.zeros((128, 128), np.float32)
    for a in range(128):
        partner = a + 1 if a % 2 == 0 else a - 1
        P[partner, a] = 1.0
    return dict(ddt=np.ascontiguousarray(ddT_sb), qdkd=qdkd, idbf=idbf, idf32=idf32,
                bd=bd, pswap=P.astype(ml_dtypes.bfloat16), tband=tband,
                rcq=(cosT * sc).astype(ml_dtypes.bfloat16),
                rsq=(sinS * sc).astype(ml_dtypes.bfloat16),
                rck=cosT.astype(ml_dtypes.bfloat16),
                rsk=sinS.astype(ml_dtypes.bfloat16))


class Prog:
    def __init__(self, nc, es):
        self.nc = nc
        self.es = es
        self.q = {e: [] for e in ENGINES}
        self.cnt = {}
        self.sems = {}
        for name in ("eT", "eV", "eS"):
            h = es.enter_context(nc.semaphore(name))
            self.sems[name] = h
            self.cnt[name] = 0
        self.bank = {b: None for b in range(8)}

    def dsem(self, name):
        if name not in self.sems:
            self.sems[name] = self.es.enter_context(self.nc.semaphore(name))
            self.cnt[name] = 0
        return name

    def sb(self, name, shape, dtype):
        es = self.scopes[-1] if getattr(self, "scopes", None) else self.es
        name = getattr(self, "prefix", "") + name
        return es.enter_context(self.nc.sbuf_tensor(name, shape, dtype))

    def ps(self, name, shape, dtype=F32):
        es = self.scopes[-1] if getattr(self, "scopes", None) else self.es
        name = getattr(self, "prefix", "") + name
        return es.enter_context(self.nc.psum_tensor(name, shape, dtype))

    def push_scope(self):
        if not hasattr(self, "scopes"):
            self.scopes = []
        self.scopes.append(contextlib.ExitStack())

    def pop_scope(self):
        self.scopes.pop().close()

    def barrier(self):
        toks = [(s, self.cnt[s]) for s in self.cnt]
        for eng in ENGINES:
            self.wait(eng, toks)
        self.bank = {b: None for b in range(8)}

    def _waits(self, after):
        w = {}
        def add(t):
            if t is None:
                return
            if t[0] == "PAIR":
                for u in t[1:]:
                    add(u)
                return
            s, v = t
            w[s] = max(w.get(s, 0), v)
        for t in (after or []):
            add(t)
        return sorted(w.items())

    def _emit(self, eng, meth, args, kw, waits, incs):
        def run(e, meth=meth, args=args, kw=kw, waits=waits, incs=incs):
            for s, v in waits:
                e.wait_ge(self.sems[s], v)
            r = getattr(e, meth)(*args, **kw)
            for s, n in incs:
                r = r.then_inc(self.sems[s], n)
        self.q[eng].append(run)

    def op(self, eng, meth, *args, after=None, done=False, **kw):
        esem = {"tensor": "eT", "vector": "eV", "scalar": "eS"}.get(eng)
        incs = []
        tok = None
        if done:
            self.cnt[esem] += 1
            incs = [(esem, 1)]
            tok = (esem, self.cnt[esem])
        self._emit(eng, meth, args, kw, self._waits(after), incs)
        return tok

    def dma(self, queue, out, in_, after=None, sem="misc", **kw):
        qsem = self.dsem(sem)
        self.cnt[qsem] += 16
        self._emit(queue, "dma_start", (), dict(out=out, in_=in_, **kw),
                   self._waits(after), [(qsem, 16)])
        return (qsem, self.cnt[qsem])

    def dma_tp(self, out, in_, after=None, sem="misc"):
        qsem = self.dsem(sem)
        self.cnt[qsem] += 16
        self._emit("sync", "dma_start_transpose", (out, in_),
                   {}, self._waits(after), [(qsem, 16)])
        return (qsem, self.cnt[qsem])

    def wait(self, eng, tokens):
        for s, v in self._waits(tokens):
            def run(e, s=s, v=v):
                e.wait_ge(self.sems[s], v)
            self.q[eng].append(run)

    def finalize(self):
        block = self.es.enter_context(self.nc.Block())
        for eng in ENGINES:
            ops = self.q[eng]
            dec = getattr(block, eng)
            def make(ops):
                def body(e):
                    for f in ops:
                        f(e)
                return body
            dec(make(ops))


def tmax(*tokens):
    out = {}
    def add(t):
        if t is None:
            return
        if t[0] == "PAIR":
            for u in t[1:]:
                add(u)
            return
        s, v = t
        out[s] = max(out.get(s, 0), v)
    for t in tokens:
        add(t)
    return [(s, v) for s, v in out.items()]


def emit_ln(p, dst, dst2, x_ap, lncol, c0, eps_ap, after, want_f32=False):
    tsum = p.op("vector", "tensor_reduce", lncol[:, c0:c0 + 1], x_ap,
                AX.X, ALU.add, after=after, done=True)
    tsq = p.op("scalar", "activation", p.junkb[:], x_ap, AF.Square,
               accum_out=lncol[:, c0 + 1:c0 + 2], after=after, done=True)
    tnm = p.op("vector", "tensor_single_scalar", lncol[:, c0 + 2:c0 + 3],
               lncol[:, c0:c0 + 1], -1.0 / 2048.0, ALU.mult, after=[tsum],
               done=True)
    tms = p.op("vector", "tensor_scalar", lncol[:, c0 + 3:c0 + 4],
               lncol[:, c0 + 2:c0 + 3], lncol[:, c0 + 2:c0 + 3], None, ALU.mult,
               after=[tnm], done=True)
    p.op("vector", "tensor_single_scalar", lncol[:, c0 + 4:c0 + 5],
         lncol[:, c0 + 1:c0 + 2], 1.0 / 2048.0, ALU.mult, after=[tsq])
    tvar = p.op("vector", "tensor_scalar", lncol[:, c0 + 5:c0 + 6],
                lncol[:, c0 + 4:c0 + 5], lncol[:, c0 + 3:c0 + 4], None,
                ALU.subtract, after=[tms], done=True)
    tsd = p.op("scalar", "activation", lncol[:, c0 + 6:c0 + 7],
               lncol[:, c0 + 5:c0 + 6], AF.Sqrt, bias=eps_ap, after=[tvar],
               done=True)
    trs = p.op("vector", "reciprocal", lncol[:, c0 + 4:c0 + 5],
               lncol[:, c0 + 6:c0 + 7], after=[tsd], done=True)
    tnr = p.op("vector", "tensor_scalar", lncol[:, c0 + 3:c0 + 4],
               lncol[:, c0 + 2:c0 + 3], lncol[:, c0 + 4:c0 + 5], None, ALU.mult,
               after=[trs], done=True)
    rstd = lncol[:, c0 + 4:c0 + 5]
    nmr = lncol[:, c0 + 3:c0 + 4]
    tok_f32 = None
    if dst2 is not None:
        tok_f32 = p.op("vector", "tensor_scalar", dst2, x_ap,
                       rstd, nmr, ALU.mult, ALU.add, after=[tnr], done=True)
    tok = None
    if dst is not None:
        tok = p.op("vector", "tensor_scalar", dst, x_ap,
                   rstd, nmr, ALU.mult, ALU.add, after=[tnr], done=True)
    return tok, tsq, tok_f32


def build_l0(p, io, MB=512, dbg=None):
    dbg = dbg or {}
    nc = p.nc
    NMB = 2048 // MB
    NT = MB // 128
    NBLK = MB // 256
    NOC = 4
    H = 32
    HPG = 4
    NHG = H // HPG

    bd = host_consts()["bd"]

    ddhg = p.sb("ddhg", [128, HPG * 2 * 256], BF16)
    qdkd = p.sb("qdkd", [128, 128], F32)
    idbf = p.sb("idbf", [128, 128], BF16)
    nwb = p.sb("nwb", [128, 2048], BF16)
    nwrow_b = p.sb("nwrow_b", [1, 2048], BF16)
    onesb = p.sb("onesb", [1, 128], BF16)
    cpool = p.sb("cpool", [128, 8], F32)
    state = p.sb("state", [64, 2048], F32)
    stbf = p.sb("stbf", [64, 2048], BF16)

    xn0T = p.sb("xn0T", [128, 16 * MB], BF16)
    attn = p.sb("attn", [128, NT * 2048], BF16)
    ga = p.sb("ga", [128, NT * 2048], BF16)
    xdqT = xn0T
    gaT = attn

    x0t = [p.sb(f"x0t{i}", [128, 2048], F32) for i in range(2)]
    xnf = p.sb("xnf", [128, 2048], F32)
    xnb = [p.sb(f"xnb{i}", [128, 2048], BF16) for i in range(2)]
    lncol = p.sb("lncol", [128, 16], F32)
    p.junkb = p.sb("junkb", [128, 2048], BF16)
    q8 = [p.sb(f"q8_{i}", [128, 2048], FP8) for i in range(2)]
    amcol = p.sb("amcol", [128, 128], F32)
    rstdc = p.sb("rstdc", [128, 8], F32)

    whead = [p.sb(f"whead{i}", [128, 2048], BF16) for i in range(2)]
    wkv = p.sb("wkv", [128, 16 * 512], BF16)
    qf = p.sb("qf", [64, HPG * MB], BF16)
    kf = p.sb("kf", [64, HPG * MB], BF16)
    kvt = p.sb("kvt", [128, NT * 512], BF16)
    awb = p.sb("awb", [128, 1024], BF16)
    tin = p.sb("tin", [128, 128], F32)
    ktkd = p.sb("ktkd", [128, 128], BF16)

    wstkA = p.sb("wstkA", [128, 16 * 512], BF16)
    wstkF = p.sb("wstkF", [128, 16 * 512], BF16)
    gateb = p.sb("gateb", [128, 512], BF16)
    nwg = p.sb("nwg", [128, 512], BF16)
    s0e = p.sb("s0e", [128, 512], F32)
    g0e = p.sb("g0e", [128, 512], F32)
    sge = p.sb("sge", [128, 512], F32)
    x1b = [p.sb(f"x1b{i}", [128, 512], BF16) for i in range(2)]
    x0sl = [p.sb(f"x0sl{i}", [128, 512], F32) for i in range(2)]

    PS = [p.ps(f"PS{b}", [128, 512], F32) for b in range(8)]
    PSbf = [ps.bitcast(BF16) for ps in PS]

    p.dma("sync", qdkd[:], io["qdkd"][:], sem="init")
    p.dma("sync", idbf[:], io["idbf"][:], sem="init")
    t_init_all = p.dma("gpsimd", nwrow_b[:], io["normw"][:], sem="init")
    t_qdkd = t_id = t_nw = t_init_all
    p.op("vector", "memset", cpool[:, 0:1], 1e-5)
    p.op("vector", "memset", cpool[:, 1:2], 1.0 / 2048.0)
    p.op("vector", "memset", state[:], 0.0)
    _t_st0 = p.op("vector", "memset", stbf[:], 0.0, done=True)
    tok_st_h = [_t_st0] * H
    t_ones = p.op("vector", "memset", onesb[:], 1.0, done=True)
    t_nwb = None
    for ocx in range(4):
        tm = p.op("tensor", "matmul", PS[3][:, 0:512], onesb[:],
                  nwrow_b[0:1, ocx * 512:(ocx + 1) * 512],
                  start=True, stop=True, after=tmax(t_ones, t_init_all, p.bank[3]),
                  done=True)
        t_nwb = p.op("vector", "tensor_copy", nwb[:, ocx * 512:(ocx + 1) * 512],
                     PS[3][:, 0:512], after=[tm], done=True)
        p.bank[3] = t_nwb

    ln_eps = cpool[:, 0:1]
    inv2048 = cpool[:, 1:2]

    sg = None
    if "gwt" in io:
        wsg = p.sb("wsg", [128, 2048], F32)
        gcl = p.sb("gcl", [128, 80], F32)
        gcc = p.sb("gcc", [128, 16], F32)
        gcr = p.sb("gcr", [1, 32], F32)
        onesf1L = p.sb("onesf1L", [1, 128], F32)
        ones128fL = p.sb("ones128fL", [128, 1], F32)
        p.op("vector", "memset", onesf1L[:], 1.0)
        t_msL = p.op("vector", "memset", ones128fL[:], 1.0, done=True)
        sgbufs = dict(wsg=wsg, gcl=gcl, gcc=gcc, gcr=gcr, onesf1=onesf1L,
                      ones128f=ones128fL, t_ms=t_msL, PSG=PS[3], gbank=3)
        sg = gamma_gen_l0(p, io, sgbufs)

    def dr(n=1):
        if sg is not None:
            for _ in range(n):
                next(sg)

    tok_wdq_store = [None, None]
    tok_q8_reader = None
    t8 = None
    tdq = None
    for rc in range(16):
        par = rc % 2
        tl = p.dma("sync", x0t[par][:], io["ffn0w"][rc * 128:(rc + 1) * 128, :],
                   after=[tok_wdq_store[par]], sem=f"x0{par}")
        am = amcol[:, 0:16]
        rc_ = amcol[:, 16:32]
        ds = amcol[:, 32:48]
        sc = amcol[:, 48:64]
        tred = p.op("vector", "tensor_reduce", am,
                    x0t[par][:].rearrange("p (b i) -> p b i", i=128),
                    AX.X, ALU.max, True, after=tmax(tl, tok_q8_reader),
                    done=True)
        trcp = p.op("vector", "reciprocal", rc_, am, after=[tred], done=True)
        tscm = p.op("vector", "tensor_single_scalar", sc, rc_, 224.0, ALU.mult,
                    after=[trcp], done=True)
        p.op("vector", "tensor_single_scalar", ds, am, 1.0 / 224.0, ALU.mult,
             after=[tred])
        for b in range(16):
            t8 = p.op("vector", "tensor_scalar", q8[0][:, b * 128:(b + 1) * 128],
                      x0t[par][:, b * 128:(b + 1) * 128], sc[:, b:b + 1],
                      None, ALU.mult,
                      after=[tok_q8_reader, tscm] if b == 0 else None,
                      done=(b == 15))
        for b in range(16):
            tdq = p.op("scalar", "activation", xnb[par][:, b * 128:(b + 1) * 128],
                       q8[0][:, b * 128:(b + 1) * 128], AF.Copy,
                       scale=ds[:, b:b + 1],
                       after=tmax(t8, tok_wdq_store[par]) if b == 0 else None,
                       done=(b == 15))
        tok_q8_reader = tdq
        tok_wdq_store[par] = p.dma(
            "sync", io["wdeq_scr"][rc * 128:(rc + 1) * 128, :], xnb[par][:],
            after=[tdq], sem=f"c0s{par}")
    t_wdq_done = ("PAIR", tok_wdq_store[0], tok_wdq_store[1])

    tok_xn0T_free = None
    tok_attn_free = None
    tok_ga_free = None
    tok_qk_war = [None] * HPG
    tok_lnbuf = [("PAIR", t8, tdq, tok_wdq_store[0]),
                 ("PAIR", t8, tdq, tok_wdq_store[1])]
    tok_cbuf = [tdq, None]
    tok_vt_war = None
    tok_tin_war = [None, None]
    tok_awb_dump = [None]
    tok_kvt_dve_war = None
    tok_ktkd_war = None
    tok_whead = [None, None]
    tok_wstkA = None
    tok_ddhg = None
    tok_wf = None
    tok_x1buf = [None, None]
    tsg = None
    last_x1_store = None

    for mb in range(NMB):
        tokbase = mb * MB

        tA = []
        for tt in range(NT):
            par = tt % 2
            row0 = tokbase + tt * 128
            tl = p.dma("sync", x0t[par][:], io["x0"][row0:row0 + 128, :],
                       after=[tok_lnbuf[par]], sem=f"x0{par}")
            tyb, tsq, _ = emit_ln(p, xnb[par][:], None, x0t[par][:], lncol,
                                  par * 8, ln_eps, [tl])
            tok_lnbuf[par] = ("PAIR", tyb, tsq)
            dr()
            for fc in range(16):
                bank = 2 + (fc % 2)
                tm = p.op("tensor", "transpose", PSbf[bank][:, 0:128],
                          xnb[par][:, fc * 128:(fc + 1) * 128], idbf[:],
                          after=tmax(tyb, t_id, p.bank[bank], tok_xn0T_free),
                          done=True)
                te = p.op("vector", "tensor_copy",
                          xn0T[:, fc * MB + tt * 128: fc * MB + (tt + 1) * 128],
                          PSbf[bank][:, 0:128], after=[tm], done=True)
                p.bank[bank] = te
                tA.append(te)
        tok_A = tA[-1]
        if mb == 0 and "dbg_xn0T" in dbg:
            td_ = p.dma("sync", dbg["dbg_xn0T"][:], xn0T[:], after=[tok_A], sem="dbg")
            tok_xn0T_free = ("PAIR", tok_xn0T_free, td_)

        tok_attn_parts = []
        for hg in range(NHG):
            t_dd = p.dma("sync", ddhg[:], io["ddt"][:, hg * 2048:(hg + 1) * 2048],
                         after=[tok_ddhg], sem="ddh")
            for fc in range(16):
                t_wv_all = p.dma("gpsimd",
                                 wkv[:, fc * 512:(fc + 1) * 512]
                                 .rearrange("p (h r) -> p h r", h=HPG),
                                 io["qkvw_t"][fc * 128:(fc + 1) * 128, :]
                                 .rearrange("p (hh q) -> p hh q", q=192)
                                 [:, hg * HPG:(hg + 1) * HPG, 64:192],
                                 after=[tok_vt_war], sem="wkv")
            te_l = [None] * NT
            for tt in range(NT):
                bank = tt % 2
                for fc in range(16):
                    p.op("tensor", "matmul", PS[bank][:, 0:512],
                         xn0T[:, fc * MB + tt * 128: fc * MB + (tt + 1) * 128],
                         wkv[:, fc * 512:(fc + 1) * 512],
                         start=(fc == 0), stop=(fc == 15),
                         after=tmax(t_wv_all, tok_A, p.bank[bank]) if fc == 0
                         else None,
                         done=(fc == 15))
                tok_mm = ("eT", p.cnt["eT"])
                te = p.op("scalar", "activation", kvt[:, tt * 512:(tt + 1) * 512],
                          PS[bank][:, 0:512], AF.Silu,
                          after=tmax(tok_mm, tok_kvt_dve_war), done=True)
                p.bank[bank] = te
                te_l[tt] = te
                dr()
            tok_vt_ready = te_l[NT - 1]

            tq_h = [None] * HPG
            tok_proj_last = [None] * HPG

            def emit_projhead(hh):
                h = hg * HPG + hh
                par = hh % 2
                qfh = qf[:, hh * MB:(hh + 1) * MB]
                tw = p.dma("gpsimd",
                           whead[par][:].rearrange("p (c o) -> p c o", c=16),
                           io["qkvw_t"][:, 192 * h:192 * h + 128]
                           .rearrange("(c p) o -> p c o", p=128),
                           after=[tok_whead[par]], sem=f"wh{par}")
                bank = par
                for fc in range(16):
                    p.op("tensor", "matmul", PS[bank][:, 0:512],
                         whead[par][:, fc * 128:(fc + 1) * 128],
                         xn0T[:, fc * MB:fc * MB + MB],
                         start=(fc == 0), stop=(fc == 15),
                         after=tmax(tw, tok_A, p.bank[bank], tok_qk_war[hh])
                         if fc == 0 else None,
                         done=(fc == 15))
                tok_mm = ("eT", p.cnt["eT"])
                tok_whead[par] = tok_mm
                tq = p.op("scalar", "activation", qfh, PS[bank][0:64, 0:512],
                          AF.Silu, after=[tok_mm], done=True)
                p.bank[bank] = tq
                tq_h[hh] = tq

            def emit_light(hh):
                nonlocal tok_ktkd_war, tok_kvt_dve_war, tok_vt_war
                nonlocal tok_attn_free
                h = hg * HPG + hh
                tq = tq_h[hh]
                for blk in range(NBLK):
                    lpar = (h * NBLK + blk) % 2
                    c0b = hh * MB + blk * 256
                    t_kf = []
                    for jc in range(2):
                        tkt = p.op("tensor", "transpose", PSbf[6][0:64, 0:128],
                                   kvt[:, (blk * 2 + jc) * 512 + hh * 128:
                                       (blk * 2 + jc) * 512 + hh * 128 + 64],
                                   idbf[:],
                                   after=tmax(te_l[blk * 2 + jc], t_id,
                                              p.bank[6], tok_qk_war[hh]),
                                   done=True)
                        tke = p.op("vector", "tensor_copy",
                                   kf[:, c0b + jc * 128: c0b + (jc + 1) * 128],
                                   PSbf[6][0:64, 0:128], after=[tkt], done=True)
                        p.bank[6] = tke
                        t_kf.append(tke)
                    t_awb = []
                    for jc in range(2):
                        bank_aw = 2 + jc
                        tm = p.op("tensor", "matmul", PS[bank_aw][:, 0:256],
                                  kf[:, c0b + jc * 128: c0b + (jc + 1) * 128],
                                  qf[:, c0b: c0b + 256],
                                  start=True, stop=True,
                                  after=tmax(tq, t_kf[jc], t_dd, p.bank[bank_aw]),
                                  done=True)
                        tdd = p.op("vector", "tensor_tensor",
                                   awb[:, lpar * 512 + jc * 256:
                                       lpar * 512 + (jc + 1) * 256],
                                   PS[bank_aw][:, 0:256],
                                   ddhg[:, (hh * 2 + jc) * 256:(hh * 2 + jc + 1) * 256],
                                   ALU.mult, after=tmax(tm, tok_awb_dump[0]),
                                   done=True)
                        p.bank[bank_aw] = tdd
                        t_awb.append(tdd)
                    for ic in range(2):
                        bank_at = 4 + ic
                        tt_g = blk * 2 + ic
                        for jc in range(2):
                            p.op("tensor", "matmul", PS[bank_at][:, 0:64],
                                 awb[:, lpar * 512 + jc * 256 + ic * 128:
                                     lpar * 512 + jc * 256 + (ic + 1) * 128],
                                 kvt[:, (blk * 2 + jc) * 512 + hh * 128 + 64:
                                     (blk * 2 + jc) * 512 + hh * 128 + 128],
                                 start=(jc == 0), stop=(jc == 1),
                                 after=tmax(t_awb[jc], te_l[blk * 2 + jc],
                                            p.bank[bank_at]),
                                 done=(jc == 1))
                        t_intra = ("eT", p.cnt["eT"])
                        tmi = p.op("tensor", "matmul", PS[6][:, 0:64],
                                   qf[:, c0b + ic * 128: c0b + (ic + 1) * 128],
                                   stbf[:, h * 64:(h + 1) * 64],
                                   start=True, stop=True,
                                   after=tmax(tq, tok_st_h[h], p.bank[6]),
                                   done=True)
                        tei = p.op("scalar", "activation",
                                   tin[:, ic * 64:(ic + 1) * 64],
                                   PS[6][:, 0:64], AF.Copy,
                                   scale=qdkd[:, h * 2 + ic: h * 2 + ic + 1],
                                   after=tmax(tmi, t_qdkd, tok_tin_war[ic]),
                                   done=True)
                        p.bank[6] = tei
                        tat = p.op("vector", "tensor_tensor",
                                   attn[:, tt_g * 2048 + h * 64:
                                        tt_g * 2048 + (h + 1) * 64],
                                   PS[bank_at][:, 0:64],
                                   tin[:, ic * 64:(ic + 1) * 64],
                                   ALU.add, after=tmax(t_intra, tei, tok_attn_free),
                                   done=True)
                        tok_tin_war[ic] = tat
                        p.bank[bank_at] = tat
                        tok_attn_parts.append(tat)
                    t_ktkd = []
                    for jc in range(2):
                        tkd = p.op("vector", "tensor_scalar",
                                   ktkd[:, jc * 64:(jc + 1) * 64],
                                   kvt[:, (blk * 2 + jc) * 512 + hh * 128:
                                       (blk * 2 + jc) * 512 + hh * 128 + 64],
                                   qdkd[:, 64 + h * 2 + jc: 64 + h * 2 + jc + 1],
                                   None, ALU.mult,
                                   after=tmax(te_l[blk * 2 + jc], t_qdkd,
                                              tok_ktkd_war),
                                   done=True)
                        t_ktkd.append(tkd)
                    for jc in range(2):
                        p.op("tensor", "matmul", PS[7][0:64, 0:64],
                             ktkd[:, jc * 64:(jc + 1) * 64],
                             kvt[:, (blk * 2 + jc) * 512 + hh * 128 + 64:
                                 (blk * 2 + jc) * 512 + hh * 128 + 128],
                             start=(jc == 0), stop=(jc == 1),
                             after=tmax(t_ktkd[0], t_ktkd[1],
                                        te_l[blk * 2 + 1],
                                        p.bank[7]) if jc == 0 else None,
                             done=(jc == 1))
                    t_smm = ("eT", p.cnt["eT"])
                    tok_ktkd_war = t_smm
                    tsu = p.op("vector", "scalar_tensor_tensor",
                               state[:, h * 64:(h + 1) * 64],
                               state[:, h * 64:(h + 1) * 64], float(bd[h]),
                               PS[7][0:64, 0:64], ALU.mult, ALU.add,
                               after=tmax(t_smm, tok_st_h[h]), done=True)
                    tok_st_h[h] = p.op("vector", "tensor_copy",
                                       stbf[:, h * 64:(h + 1) * 64],
                                       state[:, h * 64:(h + 1) * 64],
                                       after=[tsu], done=True)
                    p.bank[7] = tok_st_h[h]
                    dr()
                tok_qk_war[hh] = t_smm
                tok_proj_last[hh] = t_awb

            emit_projhead(0)
            for hh in range(HPG):
                if hh + 1 < HPG:
                    emit_projhead(hh + 1)
                emit_light(hh)
            t_awb = tok_proj_last[HPG - 1]
            if mb == 0 and hg == 0 and "dbg_qf" in dbg:
                td1_ = p.dma("sync", dbg["dbg_qf"][:], qf[:], after=[tok_st_h[H - 1]], sem="dbg")
                td2_ = p.dma("sync", dbg["dbg_kf"][:], kf[:], after=[tok_st_h[H - 1]], sem="dbg")
                td3_ = p.dma("sync", dbg["dbg_kvt"][:], kvt[:], after=[tok_st_h[H - 1]], sem="dbg")
                for _i in range(HPG):
                    tok_qk_war[_i] = ("PAIR", tok_qk_war[_i], td1_, td2_)
                tok_vt_war = ("PAIR", tok_vt_war, td3_)
                tok_kvt_dve_war = ("PAIR", tok_kvt_dve_war, td3_)
            if mb == 0 and hg == 1 and "dbg_kfl" in dbg:
                tdl_ = p.dma("sync", dbg["dbg_kfl"][:], kf[:], after=[tok_st_h[H - 1]], sem="dbg")
                for _i in range(HPG):
                    tok_qk_war[_i] = ("PAIR", tok_qk_war[_i], tdl_)
            tok_vt_war = tok_qk_war[HPG - 1]
            tok_kvt_dve_war = tok_qk_war[HPG - 1]
            tok_ddhg = t_awb[-1]
        tok_attn_done = tok_attn_parts[-1]
        if mb == 0 and "dbg_attn" in dbg:
            tda_ = p.dma("sync", dbg["dbg_attn"][:], attn[:], after=[tok_attn_done], sem="dbg")
            tok_attn_dump = tda_
        else:
            tok_attn_dump = None

        trec = []
        for tt in range(NT):
            tsq = p.op("scalar", "activation", p.junkb[:],
                       attn[:, tt * 2048:(tt + 1) * 2048], AF.Square,
                       accum_out=lncol[:, tt:tt + 1],
                       after=[tok_attn_done], done=True)
            tsd = p.op("scalar", "activation", lncol[:, 8 + tt: 9 + tt],
                       lncol[:, tt:tt + 1], AF.Sqrt, bias=ln_eps, scale=inv2048,
                       after=[tsq], done=True)
            tr = p.op("vector", "reciprocal", rstdc[:, tt:tt + 1],
                      lncol[:, 8 + tt:9 + tt], after=[tsd], done=True)
            trec.append(tr)

        tok_wg = tok_wstkA
        tga = None
        tn = None
        tga_l = [[None] * NT for _ in range(NOC)]
        for oc in range(NOC):
            twl = p.dma("gpsimd", wstkA[:].rearrange("p (c o) -> p c o", c=16),
                        io["gatew_t"][:, oc * 512:(oc + 1) * 512]
                        .rearrange("(c p) o -> p c o", p=128), after=[tok_wg],
                        sem="wsa")
            for tt in range(NT):
                bank = tt % 2
                for fc in range(16):
                    p.op("tensor", "matmul", PS[bank][:, 0:512],
                         xn0T[:, fc * MB + tt * 128: fc * MB + (tt + 1) * 128],
                         wstkA[:, fc * 512:(fc + 1) * 512],
                         start=(fc == 0), stop=(fc == 15),
                         after=tmax(twl, tok_A, p.bank[bank]) if fc == 0 else None,
                         done=(fc == 15))
                tok_mm = ("eT", p.cnt["eT"])
                tg = p.op("scalar", "activation", gateb[:], PS[bank][:, 0:512],
                          AF.Sigmoid, after=tmax(tok_mm, tn), done=True)
                p.bank[bank] = tg
                tn = p.op("vector", "tensor_tensor", nwg[:], gateb[:],
                          nwb[:, oc * 512:(oc + 1) * 512], ALU.mult,
                          after=tmax(tg, t_nwb), done=True)
                tga = p.op("vector", "scalar_tensor_tensor",
                           ga[:, tt * 2048 + oc * 512: tt * 2048 + (oc + 1) * 512],
                           attn[:, tt * 2048 + oc * 512: tt * 2048 + (oc + 1) * 512],
                           rstdc[:, tt:tt + 1], nwg[:], ALU.mult, ALU.mult,
                           after=tmax(tn, trec[tt], tok_ga_free), done=True)
                tga_l[oc][tt] = tga
                dr()
            tok_wg = tok_mm
        tok_gate_done = tok_mm

        t_gaT = []
        gaT_tt = [None] * NT
        for tt in range(NT):
            for ac in range(16):
                bank = 2 + (ac % 2)
                tm = p.op("tensor", "transpose", PSbf[bank][:, 0:128],
                          ga[:, tt * 2048 + ac * 128: tt * 2048 + (ac + 1) * 128],
                          idbf[:],
                          after=tmax(tga_l[ac // 4][tt], p.bank[bank],
                                     tok_attn_dump),
                          done=True)
                te = p.op("vector", "tensor_copy",
                          gaT[:, ac * MB + tt * 128: ac * MB + (tt + 1) * 128],
                          PSbf[bank][:, 0:128], after=[tm], done=True)
                p.bank[bank] = te
                t_gaT.append(te)
            gaT_tt[tt] = te
        tok_gaT = t_gaT[-1]
        tok_ga_free = tok_gaT
        if mb == 0 and "dbg_ga" in dbg:
            tdg_ = p.dma("sync", dbg["dbg_ga"][:], ga[:], after=[tok_gaT], sem="dbg")
            tok_ga_free = ("PAIR", tok_ga_free, tdg_)

        t_xdq = []
        for tt in range(NT):
            par = tt % 2
            row0 = tokbase + tt * 128
            tl = p.dma("sync", x0t[par][:], io["x0"][row0:row0 + 128, :],
                       after=[tok_lnbuf[par]])
            _, tsq2, txn = emit_ln(p, None, xnf[:], x0t[par][:], lncol,
                                   par * 8, ln_eps, [tl], want_f32=True)
            tok_lnbuf[par] = ("PAIR", txn, tsq2)
            am = amcol[:, par * 64: par * 64 + 16]
            rcp_ = amcol[:, par * 64 + 16: par * 64 + 32]
            sc224 = amcol[:, par * 64 + 32: par * 64 + 48]
            tred = p.op("vector", "tensor_reduce", am,
                        xnf[:].rearrange("p (b i) -> p b i", i=128),
                        AX.X, ALU.max, True, after=[txn, tok_cbuf[par]],
                        done=True)
            trcp = p.op("vector", "reciprocal", rcp_, am, after=[tred], done=True)
            tsc = p.op("vector", "tensor_single_scalar", sc224, rcp_, 224.0,
                       ALU.mult, after=[trcp], done=True)
            t8x = None
            for b in range(16):
                t8x = p.op("vector", "tensor_scalar",
                           q8[par][:, b * 128:(b + 1) * 128],
                           xnf[:, b * 128:(b + 1) * 128], sc224[:, b:b + 1],
                           None, ALU.mult,
                           after=[tok_cbuf[par], tsc] if b == 0 else None,
                           done=(b == 15))
            tds = p.op("vector", "tensor_single_scalar", sc224, am, 1.0 / 224.0,
                       ALU.mult, after=[tred], done=True)
            tdq_l = []
            for b in range(16):
                tdqx = p.op("scalar", "activation",
                            xnb[par][:, b * 128:(b + 1) * 128],
                            q8[par][:, b * 128:(b + 1) * 128], AF.Copy,
                            scale=sc224[:, b:b + 1],
                            after=tmax(t8x, tds, tok_A, tok_lnbuf[par])
                            if b == 0 else None,
                            done=True)
                tdq_l.append(tdqx)
            tok_cbuf[par] = tdqx
            dr()
            for fc in range(16):
                bank = 2 + (fc % 2)
                tm = p.op("tensor", "transpose", PSbf[bank][:, 0:128],
                          xnb[par][:, fc * 128:(fc + 1) * 128], idbf[:],
                          after=tmax(tdq_l[fc], tok_gate_done, p.bank[bank]),
                          done=True)
                te = p.op("vector", "tensor_copy",
                          xdqT[:, fc * MB + tt * 128: fc * MB + (tt + 1) * 128],
                          PSbf[bank][:, 0:128], after=[tm], done=True)
                p.bank[bank] = te
                t_xdq.append(te)
        tok_xdq = t_xdq[-1]

        tok_wo = tok_gate_done
        tok_wf = None
        tok_x1buf = [None, None]
        tsg = None
        for oc in range(NOC):
            two = p.dma("gpsimd", wstkA[:].rearrange("p (c o) -> p c o", c=16),
                        io["outw_t"][:, oc * 512:(oc + 1) * 512]
                        .rearrange("(c p) o -> p c o", p=128), after=[tok_wo],
                        sem="wsa")
            for fc in range(16):
                t_wfl_all = p.dma_tp(wstkF[:, fc * 512:(fc + 1) * 512],
                                     io["wdeq_scr"][oc * 512:(oc + 1) * 512,
                                                    fc * 128:(fc + 1) * 128],
                                     after=[t_wdq_done, tok_wf], sem="wsf")
            for tt in range(NT):
                parx = tt % 2
                row0 = tokbase + tt * 128
                bs = tt % 2
                bg = 4 + tt % 2
                for ac in range(16):
                    p.op("tensor", "matmul", PS[bs][:, 0:512],
                         gaT[:, ac * MB + tt * 128: ac * MB + (tt + 1) * 128],
                         wstkA[:, ac * 512:(ac + 1) * 512],
                         start=(ac == 0), stop=(ac == 15),
                         after=tmax(two, gaT_tt[tt], p.bank[bs])
                         if ac == 0 else None,
                         done=(ac == 15))
                t_s0 = ("eT", p.cnt["eT"])
                for fc in range(16):
                    p.op("tensor", "matmul", PS[bg][:, 0:512],
                         xdqT[:, fc * MB + tt * 128: fc * MB + (tt + 1) * 128],
                         wstkF[:, fc * 512:(fc + 1) * 512],
                         start=(fc == 0), stop=(fc == 15),
                         after=tmax(t_wfl_all, tok_xdq, p.bank[bg])
                         if fc == 0 else None,
                         done=(fc == 15))
                t_g0 = ("eT", p.cnt["eT"])
                tse = p.op("scalar", "activation", s0e[:], PS[bs][:, 0:512],
                           AF.Copy, after=tmax(t_s0, tsg), done=True)
                p.bank[bs] = tse
                tge = p.op("scalar", "activation", g0e[:], PS[bg][:, 0:512],
                           AF.Silu, after=tmax(t_g0, tsg), done=True)
                p.bank[bg] = tge
                txl = p.dma("sync", x0sl[parx][:],
                            io["x0"][row0:row0 + 128, oc * 512:(oc + 1) * 512],
                            after=[tok_x1buf[parx]], sem=f"xs{parx}")
                tsg = p.op("vector", "tensor_tensor", sge[:], s0e[:], g0e[:],
                           ALU.mult, after=tmax(tse, tge), done=True)
                tx1 = p.op("vector", "tensor_tensor", x1b[parx][:], sge[:],
                           x0sl[parx][:], ALU.add, after=tmax(tsg, txl), done=True)
                ts = p.dma("sync", io["x1"][row0:row0 + 128, oc * 512:(oc + 1) * 512],
                           x1b[parx][:], after=[tx1], sem=f"x1s{parx}")
                tok_x1buf[parx] = ts
                last_x1_store = ts
                dr()
            tok_wo = t_s0
            tok_wf = t_g0
        tok_wstkA = t_s0
        tok_xn0T_free = t_g0
        tok_attn_free = t_s0

    dr(100)
    p.wait("sync", [tok_x1buf[0], tok_x1buf[1]])
    return last_x1_store





def gamma_gen_l0(p, io, bufs):
    wsg = bufs["wsg"]; gcl = bufs["gcl"]; gcc = bufs["gcc"]; gcr = bufs["gcr"]
    onesf1 = bufs["onesf1"]; ones128f = bufs["ones128f"]
    t_ms = bufs["t_ms"]; PSG = bufs["PSG"]; GB = bufs["gbank"]
    HW = 1024
    tok_sl = [None, None]
    gammas = []
    tok_red = None
    tok_gb = []
    for mi, (wkey, ntile, width) in enumerate(
            [("gwt", 16, 8192), ("vwt", 16, 8192), ("owt", 64, 2048)]):
        nchunk = width // HW
        cnt = 0
        npart = 0
        for i in range(ntile):
            for ch in range(nchunk):
                par = cnt % 2
                tile = wsg[:, par * HW:(par + 1) * HW]
                tl = p.dma("sync", tile,
                           io[wkey][i * 128:(i + 1) * 128,
                                    ch * HW:(ch + 1) * HW],
                           after=[tok_sl[par]], sem=f"sg{par}")
                tok_sl[par] = p.op("vector", "tensor_reduce",
                                   gcl[:, cnt % 64:cnt % 64 + 1],
                                   tile, AX.X, ALU.add, True,
                                   after=tmax(tl, tok_red if cnt % 64 == 0
                                              else None), done=True)
                cnt += 1
                if cnt % 64 == 0:
                    tok_red = p.op("vector", "tensor_reduce",
                                   gcl[:, 64 + npart:65 + npart],
                                   gcl[:, 0:64], AX.X, ALU.add,
                                   after=[tok_sl[0], tok_sl[1]], done=True)
                    npart += 1
                yield
        tmt = p.op("vector", "tensor_reduce", gcl[:, 70 + mi:71 + mi],
                   gcl[:, 64:64 + npart], AX.X, ALU.add,
                   after=[tok_red], done=True)
        tpsum = p.op("tensor", "matmul", PSG[0:1, 0:1],
                     gcl[:, 70 + mi:71 + mi], ones128f[:], start=True,
                     stop=True, after=tmax(tmt, t_ms, p.bank[GB]), done=True)
        t1_ = p.op("vector", "tensor_single_scalar",
                   gcr[0:1, 8 + mi * 3:9 + mi * 3],
                   PSG[0:1, 0:1], 1.0 / 16777216.0, ALU.mult, after=[tpsum],
                   done=True)
        p.bank[GB] = t1_
        t2_ = p.op("vector", "tensor_single_scalar",
                   gcr[0:1, 9 + mi * 3:10 + mi * 3],
                   gcr[0:1, 8 + mi * 3:9 + mi * 3], 1e-5, ALU.max,
                   after=[t1_], done=True)
        t3_ = p.op("vector", "reciprocal", gcr[0:1, 10 + mi * 3:11 + mi * 3],
                   gcr[0:1, 9 + mi * 3:10 + mi * 3], after=[t2_], done=True)
        gammas.append((9 + mi * 3, 10 + mi * 3, t2_, t3_))
        yield
    for mi in range(3):
        gcolx, grcolx, tg_, tr_ = gammas[mi]
        tmmg = p.op("tensor", "matmul", PSG[:, 0:2],
                    onesf1[:], gcr[0:1, gcolx:gcolx + 2], start=True,
                    stop=True, after=tmax(tg_, tr_, t_ms, p.bank[GB]),
                    done=True)
        tevg = p.op("vector", "tensor_copy", gcc[:, 8 + mi * 2:10 + mi * 2],
                    PSG[:, 0:2], after=[tmmg], done=True)
        p.bank[GB] = tevg
        tok_gb.append(tevg)
        yield
    tok_gsc = p.dma("sync", io["gsc"][:, 0:16], gcc[:, 0:16],
                    after=tok_gb, sem="gsc")

    brs = wsg[0:1, 0:HW]
    bjunk = wsg.bitcast(BF16)[0:1, 2 * HW:3 * HW]
    tok_bs = ("PAIR", tok_sl[0], tok_sl[1])
    for mi, bkey, brow in [(0, "gb", 0), (1, "vb", 1)]:
        for ch in range(8):
            tbl = p.dma("sync", brs, io[bkey][0:1, ch * HW:(ch + 1) * HW],
                        after=[tok_bs], sem="bias")
            tsc_ = p.op("vector", "tensor_scalar", bjunk,
                        brs, gcr[0:1, gammas[mi][1]:gammas[mi][1] + 1],
                        None, ALU.mult, after=tmax(tbl, gammas[mi][3]),
                        done=True)
            tok_bs = p.dma("sync",
                           io["wbias"][brow:brow + 1, ch * HW:(ch + 1) * HW],
                           bjunk, after=[tsc_], sem="bias")
            yield
    for ch in range(2):
        tbl3 = p.dma("sync", brs, io["ob"][0:1, ch * HW:(ch + 1) * HW],
                     after=[tok_bs], sem="bias")
        tbo_ = p.op("vector", "tensor_scalar", bjunk, brs,
                    gcr[0:1, gammas[2][1]:gammas[2][1] + 1], None, ALU.mult,
                    after=tmax(tbl3, gammas[2][3]), done=True)
        tok_bs = p.dma("sync", io["wbias"][2:3, ch * HW:(ch + 1) * HW],
                       bjunk, after=[tbo_], sem="bias")
        yield
    bufs["tok_done"] = ("PAIR", tok_gsc, tok_bs)
    while True:
        yield

def gamma_quant_gen(p, io, bufs):
    wga = bufs["wga"]; q8g = bufs["q8g"]; j8g = bufs["j8g"]
    gccol = bufs["gccol"]

    tok_gs = p.dma("sync", gccol[:, 0:16], io["gsc"][:, 0:16], sem="gsc")
    GRg = lambda mi: gccol[:, 9 + mi * 2: 10 + mi * 2]
    tok_gt = [None, None]
    tok_wq8 = [None, None]
    for mi, (wkey, skey, ntile, width) in enumerate(
            [("gwt", "wgi8", 16, 8192), ("vwt", "wvi8", 16, 8192),
             ("owt", "woi8", 64, 2048)]):
        nhalf = 2 if width == 8192 else 1
        hw = width // nhalf
        cnt = 0
        for i in range(ntile):
            for half in range(nhalf):
                par = cnt % 2
                cnt += 1
                tile = wga[:, par * 4096: par * 4096 + hw]
                tl = p.dma("sync", tile,
                           io[wkey][i * 128:(i + 1) * 128,
                                    half * hw:(half + 1) * hw],
                           after=[tok_gt[par], tok_wq8[par]], sem=f"gt{par}")
                tq1 = p.op("vector", "tensor_scalar", q8g[:, 0:hw], tile,
                           GRg(mi), None, ALU.mult,
                           after=tmax(tl, tok_gs, tok_wq8[par]), done=True)
                tq2 = p.op("vector", "tensor_scalar", j8g[:, 0:hw],
                           q8g[:, 0:hw], 1.0, -1.0, ALU.min, ALU.max,
                           after=[tq1], done=True)
                tok_gt[par] = tq2
                tok_wq8[par] = p.dma(
                    "sync",
                    io[skey][i * 128:(i + 1) * 128, half * hw:(half + 1) * hw],
                    j8g[:, 0:hw], after=[tq2], sem=f"w8s{par}")
                yield
    bufs["tok_w8_done"] = ("PAIR", tok_wq8[0], tok_wq8[1])
    while True:
        yield


def build_l1a(p, io, dbg=None, side=True):
    dbg = dbg or {}
    NH = 16

    xn1T = p.sb("xn1T", [128, 16 * 2048], BF16)
    qh = [p.sb(f"qh{i}", [128, 2048], BF16) for i in range(2)]
    mall2 = p.sb("mall", [1, 4096], BF16)
    kro = p.sb("kro", [128, 2048], BF16)
    vsb = p.sb("vsb", [128, 16 * 136], BF16)
    ksb = p.sb("ksb", [128, 2048], BF16)
    tband = p.sb("s_tband", [128, 1024], BF16)
    rcq = p.sb("s_rcq", [128, 2048], BF16)
    rsq = p.sb("s_rsq", [128, 2048], BF16)
    rck = p.sb("s_rck", [128, 2048], BF16)
    rsk = p.sb("s_rsk", [128, 2048], BF16)
    pswap = p.sb("s_pswap", [128, 128], BF16)
    idbf = p.sb("s_idbf", [128, 128], BF16)
    idf32 = p.sb("s_idf32", [128, 128], F32)
    wkb = p.sb("wkb", [128, 2048], BF16)
    wvb = p.sb("wvb", [128, 2048], BF16)
    wqb = [p.sb(f"wqb{i}", [128, 2048], BF16) for i in range(2)]
    tmp1 = p.sb("tmp1", [128, 512], F32)
    tmp2 = p.sb("tmp2", [128, 512], F32)
    sqt = p.sb("sqt", [128, 512], BF16)
    krow = p.sb("krow", [1, 2048], BF16)
    nq2 = p.sb("nq2", [1, 4096], BF16)
    crow = p.sb("crow", [1, 8], F32)
    onesb = p.sb("onesb1", [1, 128], BF16)
    negob = p.sb("negob", [1, 128], BF16)
    ones128b = p.sb("ones128b", [128, 1], BF16)
    cpool = p.sb("cpool1", [128, 8], F32)
    lncol = p.sb("lncol1", [128, 16], F32)
    p.junkb = p.sb("junkb1", [128, 2048], BF16)
    psb_ = [p.sb(f"psb{i}", [128, 512], BF16) for i in range(4)]
    pmk2 = p.sb("pmk", [128, 1024], BF16)
    s1t = [p.sb(f"s1t{i}", [128, 512], BF16) for i in range(2)]
    rcol = p.sb("rcol", [128, 8], F32)
    x1t = [p.sb(f"x1t{i}", [128, 2048], BF16) for i in range(2)]
    xnb1 = [p.sb(f"xnb1_{i}", [128, 2048], BF16) for i in range(2)]
    wga = p.sb("wga", [128, 2 * 4096], F32)
    q8g = x1t[0].bitcast(I8)
    j8g = x1t[1].bitcast(I8)
    gccol = p.sb("gccol", [128, 16], F32)

    PS = [p.ps(f"PT{b}", [128, 512], F32) for b in range(8)]
    PSbf = [ps.bitcast(BF16) for ps in PS]
    ln_eps = cpool[:, 0:1]

    p.dma("sync", tband[:], io["tband"][:], sem="init")
    p.dma("sync", rcq[:], io["rcq"][:], sem="init")
    p.dma("sync", rsq[:], io["rsq"][:], sem="init")
    p.dma("sync", rck[:], io["rck"][:], sem="init")
    p.dma("sync", rsk[:], io["rsk"][:], sem="init")
    p.dma("sync", pswap[:], io["pswap"][:], sem="init")
    p.dma("sync", idbf[:], io["idbf"][:], sem="init")
    p.dma("sync", idf32[:], io["idf32"][:], sem="init")
    p.dma("gpsimd", wkb[:].rearrange("p (c o) -> p c o", c=16),
          io["wk"][:].rearrange("(c p) o -> p c o", p=128), sem="init")
    tok_init = p.dma("gpsimd", wvb[:].rearrange("p (c o) -> p c o", c=16),
                     io["wv"][:].rearrange("(c p) o -> p c o", p=128), sem="init")
    p.op("vector", "memset", cpool[:, 0:1], 1e-5)
    p.op("vector", "memset", onesb[:], 1.0)
    p.op("vector", "memset", negob[:], -1.0)
    t_msv = p.op("vector", "memset",
                 vsb[:].rearrange("p (t c) -> p t c", c=136)[:, :, 128:129],
                 1.0, done=True)
    t_ms = p.op("vector", "memset", ones128b[:], 1.0, done=True)
    gbufs = dict(wga=wga, q8g=q8g, j8g=j8g, gccol=gccol)
    gq = gamma_quant_gen(p, io, gbufs) if side else None

    tok_lnb = [None, None]
    tA = None
    for tt in range(16):
        par = tt % 2
        tl = p.dma("sync", x1t[par][:], io["x1"][tt * 128:(tt + 1) * 128, :],
                   after=[tok_lnb[par]], sem=f"x1l{par}")
        tyb, tsq, _ = emit_ln(p, xnb1[par][:], None, x1t[par][:], lncol,
                              par * 8, ln_eps, [tl])
        tok_lnb[par] = ("PAIR", tyb, tsq)
        if gq is not None:
            next(gq)
        for fc in range(16):
            bank = 2 + (fc % 2)
            tm = p.op("tensor", "transpose", PSbf[bank][:, 0:128],
                      xnb1[par][:, fc * 128:(fc + 1) * 128], idbf[:],
                      after=tmax(tyb, tok_init, p.bank[bank]), done=True)
            tA = p.op("vector", "tensor_copy",
                      xn1T[:, fc * 2048 + tt * 128: fc * 2048 + (tt + 1) * 128],
                      PSbf[bank][:, 0:128], after=[tm], done=True)
            p.bank[bank] = tA
    tok_xn1T = tA

    def rope_combine(dst, raw_ap, swap_ps, ct, st, c512, after_toks):
        ta = p.op("vector", "tensor_tensor", tmp1[:], raw_ap,
                  ct[:, c512 * 512:(c512 + 1) * 512], ALU.mult,
                  after=after_toks, done=True)
        tb = p.op("vector", "tensor_tensor", tmp2[:], swap_ps,
                  ct[:, 0:0] if False else st[:, c512 * 512:(c512 + 1) * 512],
                  ALU.mult, after=after_toks, done=True)
        return p.op("vector", "tensor_tensor", dst, tmp1[:], tmp2[:],
                    ALU.add, after=tmax(ta, tb), done=True)

    tok_kevac = [None] * 4
    tok_krope = [None] * 4
    tok_sqt = None
    tok_nk = None
    for kc in range(4):
        bank = kc % 2
        for fc in range(16):
            p.op("tensor", "matmul", PS[bank][:, 0:512],
                 wkb[:, fc * 128:(fc + 1) * 128],
                 xn1T[:, fc * 2048 + kc * 512: fc * 2048 + (kc + 1) * 512],
                 start=(fc == 0), stop=(fc == 15),
                 after=tmax(tok_init, tok_xn1T, p.bank[bank]) if fc == 0
                 else None, done=(fc == 15))
        tmm = ("eT", p.cnt["eT"])
        te = p.op("scalar", "activation", ksb[:, kc * 512:(kc + 1) * 512],
                  PS[bank][:, 0:512], AF.Copy, after=[tmm], done=True)
        p.bank[bank] = te
        tok_kevac[kc] = te
    for kc in range(4):
        bank = kc % 2
        tsw = p.op("tensor", "matmul", PS[bank][:, 0:512], pswap[:],
                   ksb[:, kc * 512:(kc + 1) * 512], start=True, stop=True,
                   after=tmax(tok_kevac[kc], tok_init, p.bank[bank]), done=True)
        trp = rope_combine(kro[:, kc * 512:(kc + 1) * 512],
                           ksb[:, kc * 512:(kc + 1) * 512],
                           PS[bank][:, 0:512], rck, rsk, kc,
                           tmax(tsw, tok_kevac[kc], tok_init))
        p.bank[bank] = trp
        tok_krope[kc] = trp
        tsq_ = p.op("scalar", "activation", sqt[:],
                    kro[:, kc * 512:(kc + 1) * 512], AF.Square,
                    after=tmax(trp, tok_sqt), done=True)
        tnm = p.op("tensor", "matmul", PS[2][0:1, 0:512], ones128b[:],
                   sqt[:], start=True, stop=True,
                   after=tmax(tsq_, t_ms, p.bank[2]), done=True)
        tok_sqt = tnm
        tok_nk = p.op("vector", "tensor_copy",
                      krow[0:1, kc * 512:(kc + 1) * 512],
                      PS[2][0:1, 0:512], after=[tnm], done=True)
        p.bank[2] = tok_nk
    tgm = p.op("vector", "tensor_reduce", crow[0:1, 0:1], krow[0:1, 0:2048],
               AX.X, ALU.max, after=[tok_nk], done=True)
    tG = p.op("scalar", "activation", crow[0:1, 1:2], crow[0:1, 0:1], AF.Sqrt,
              after=[tgm], done=True)

    for tt in range(16):
        bank = tt % 2
        for fc in range(16):
            p.op("tensor", "matmul", PS[bank][:, 0:128],
                 xn1T[:, fc * 2048 + tt * 128: fc * 2048 + (tt + 1) * 128],
                 wvb[:, fc * 128:(fc + 1) * 128],
                 start=(fc == 0), stop=(fc == 15),
                 after=tmax(tok_init, tok_xn1T, p.bank[bank]) if fc == 0
                 else None, done=(fc == 15))
        tmm = ("eT", p.cnt["eT"])
        tok_v = p.op("scalar", "activation",
                     vsb[:, tt * 136:tt * 136 + 128],
                     PS[bank][:, 0:128], AF.Copy, after=[tmm], done=True)
        p.bank[bank] = tok_v

    tok_wq = [None, None]
    tok_ksbq = list(tok_krope)
    tok_nq_reader = [None, None]
    tok_m_reader = [None, None]
    tok_m_h = [None, None]
    tok_qrope_h = [None, None]
    tok_qh = [None, None]
    tok_psb = [None, None, None, None]
    tok_pmk = [None, None]
    tok_rcol_war = None
    tok_s1_store = [None, None]
    tw_h = [None, None]
    tok_sqt_ref = [tok_sqt]

    def emit_proj_qc(h, qc):
        par = h % 2
        b = qc % 2
        for fc in range(16):
            p.op("tensor", "matmul", PS[b][:, 0:512],
                 wqb[par][:, fc * 128:(fc + 1) * 128],
                 xn1T[:, fc * 2048 + qc * 512: fc * 2048 + (qc + 1) * 512],
                 start=(fc == 0), stop=(fc == 15),
                 after=tmax(tw_h[par], tok_xn1T, p.bank[b]) if fc == 0
                 else None, done=(fc == 15))
        tmm = ("eT", p.cnt["eT"])
        if qc == 3:
            tok_wq[par] = tmm
        te = p.op("vector", "tensor_copy", ksb[:, qc * 512:(qc + 1) * 512],
                  PS[b][:, 0:512],
                  after=tmax(tmm, tok_ksbq[qc]), done=True)
        p.bank[b] = te
        tsw = p.op("tensor", "matmul", PS[b][:, 0:512], pswap[:],
                   ksb[:, qc * 512:(qc + 1) * 512], start=True, stop=True,
                   after=tmax(te, tok_init, p.bank[b]), done=True)
        trp = rope_combine(
            qh[par][:, qc * 512:(qc + 1) * 512],
            ksb[:, qc * 512:(qc + 1) * 512],
            PS[b][:, 0:512], rcq, rsq, qc,
            tmax(tsw, te, tok_qh[par]))
        tok_ksbq[qc] = trp
        tsq_ = p.op("vector", "tensor_tensor", sqt[:],
                    qh[par][:, qc * 512:(qc + 1) * 512],
                    qh[par][:, qc * 512:(qc + 1) * 512],
                    ALU.mult, after=tmax(trp, tok_sqt_ref[0]), done=True)
        tnm = p.op("tensor", "matmul", PS[b][0:1, 0:512], ones128b[:],
                   sqt[:], start=True, stop=True,
                   after=tmax(tsq_, t_ms, trp), done=True)
        tok_sqt_ref[0] = tnm
        tq_ = p.op("vector", "tensor_copy",
                   nq2[0:1, par * 2048 + qc * 512:
                       par * 2048 + (qc + 1) * 512],
                   PS[b][0:1, 0:512],
                   after=tmax(tnm, tok_nq_reader[par]), done=True)
        p.bank[b] = tq_
        return trp, tq_

    def emit_attn_qc(h, qc):
        nonlocal tok_rcol_war
        par = h % 2
        tok_qrope = tok_qrope_h[par]
        tok_m = tok_m_h[par]
        nkt = 4 * qc + 4
        tsm_l = [None] * nkt
        tok_last_mm = None

        def emit_S(kt):
            bank = 2 + (kt % 2)
            p.op("tensor", "matmul", PS[bank][:, 0:512],
                 kro[:, kt * 128:(kt + 1) * 128],
                 qh[par][:, qc * 512:(qc + 1) * 512],
                 start=True, stop=False,
                 after=tmax(tok_qrope, tok_m, p.bank[bank]))
            p.op("tensor", "matmul", PS[bank][:, 0:512], negob[:],
                 mall2[0:1, par * 2048 + qc * 512:
                       par * 2048 + (qc + 1) * 512],
                 start=False, stop=True, done=True)
            tsm_l[kt] = ("eT", p.cnt["eT"])

        emit_S(0)
        for kt in range(nkt):
            bank = 2 + (kt % 2)
            s = kt % 4
            texp = p.op("scalar", "activation", psb_[s][:],
                        PS[bank][:, 0:512], AF.Exp,
                        after=tmax(tsm_l[kt], tok_psb[s]), done=True)
            p.bank[bank] = texp
            src_p = psb_[s]
            tok_p = texp
            o = kt - 4 * qc
            if o >= 0:
                m2 = kt % 2
                tok_p = p.op("vector", "tensor_tensor",
                             pmk2[:, m2 * 512:(m2 + 1) * 512],
                             psb_[s][:],
                             tband[:, 384 - 128 * o: 896 - 128 * o],
                             ALU.mult,
                             after=tmax(texp, tok_init, tok_pmk[m2]),
                             done=True)
                src_p = pmk2[:, m2 * 512:(m2 + 1) * 512]
            if kt + 1 < nkt:
                emit_S(kt + 1)
            for qt in range(4):
                p.op("tensor", "matmul", PS[4 + qt][:, 0:129],
                     src_p[:, qt * 128:(qt + 1) * 128] if o >= 0
                     else src_p[:, qt * 128:(qt + 1) * 128],
                     vsb[:, kt * 136:kt * 136 + 129],
                     start=(kt == 0), stop=(kt == nkt - 1),
                     after=tmax(tok_p, t_msv,
                                p.bank[4 + qt] if kt == 0 else None),
                     done=(qt == 3))
            tok_last_mm = ("eT", p.cnt["eT"])
            if o >= 0:
                tok_pmk[kt % 2] = tok_last_mm
            tok_psb[s] = tok_last_mm
        par2 = (h * 4 + qc) % 2
        tsc_l = []
        for qt in range(4):
            trc = p.op("vector", "reciprocal", rcol[:, qt:qt + 1],
                       PS[4 + qt][:, 128:129],
                       after=tmax(tok_last_mm, tok_rcol_war), done=True)
            tsc = p.op("vector", "tensor_scalar",
                       s1t[par2][:, qt * 128:(qt + 1) * 128],
                       PS[4 + qt][:, 0:128], rcol[:, qt:qt + 1], None,
                       ALU.mult,
                       after=tmax(trc, tok_s1_store[par2]),
                       done=True)
            p.bank[4 + qt] = tsc
            tsc_l.append(tsc)
        tok_rcol_war = tsc_l[-1]
        tok_s1_store[par2] = p.dma(
            "sync",
            io["s1"][qc * 512:(qc + 1) * 512, h * 128:(h + 1) * 128]
            .rearrange("(qt pp) d -> pp qt d", pp=128),
            s1t[par2][:].rearrange("pp (qt d) -> pp qt d", qt=4),
            after=tsc_l, sem=f"s1s{par2}")
        return tok_last_mm

    for hi in range(NH + 1):
        par = hi % 2
        if hi < NH:
            tw_h[par] = p.dma(
                "gpsimd", wqb[par][:].rearrange("p (c o) -> p c o", c=16),
                io["wq"][:, hi * 128:(hi + 1) * 128]
                .rearrange("(c p) o -> p c o", p=128),
                after=[tok_wq[par]], sem=f"wq{par}")
        trp_last = None
        tq_l = []
        for qc in range(4):
            if hi < NH:
                trp_last, tq_ = emit_proj_qc(hi, qc)
                tq_l.append(tq_)
            if hi > 0:
                tok_last = emit_attn_qc(hi - 1, qc)
            if gq is not None:
                for _ in range(2):
                    next(gq)
        if hi > 0:
            apar = (hi - 1) % 2
            tok_qh[apar] = tok_last
            tok_m_reader[apar] = tok_last
        if hi < NH:
            tqn = p.op("scalar", "activation",
                       mall2[0:1, par * 2048:(par + 1) * 2048],
                       nq2[0:1, par * 2048:(par + 1) * 2048], AF.Sqrt,
                       scale=crow[0:1, 0:1],
                       after=tmax(*tq_l, tgm, tok_m_reader[par]), done=True)
            tok_nq_reader[par] = tqn
            tok_m_h[par] = tqn
            tok_qrope_h[par] = trp_last
        if gq is not None:
            next(gq)
    if gq is not None:
        for _ in range(20):
            next(gq)
    p.wait("sync", [tok_s1_store[0], tok_s1_store[1]])
    return tok_s1_store


def build_l1b(p, io, dbg=None):
    dbg = dbg or {}
    TG = 512
    NG = 2048 // TG
    NTT = TG // 128

    wga = p.sb("wga", [128, 2 * 4096], F32)
    wtb = [wga.bitcast(BF16)[:, i * 8192:(i + 1) * 8192] for i in range(2)]
    gvsb = p.sb("gvsb", [128, 4 * 8192], BF16)
    xqT = p.sb("xqT", [128, 16 * 512], BF16)
    gsb = p.sb("gsb", [128, 4 * 512], BF16)
    vsb2 = p.sb("vsb2", [128, 4 * 512], BF16)
    xqrow = p.sb("xqrow", [1, 512], BF16)
    r3row = p.sb("r3row", [1, 512], BF16)
    borow = p.sb("borow", [1, 2048], BF16)
    bgv = p.sb("bgv", [1, 2048], BF16)
    q83 = p.sb("q83", [128, 8192], I8)
    xq3b = p.sb("xq3b", [128, 8192], BF16)
    g1e = p.sb("g1e", [128, 512], F32)
    sg1 = p.sb("sg1", [128, 512], F32)
    yt = [p.sb(f"yt{i}", [128, 512], F32) for i in range(2)]
    x1t = [p.sb(f"x1t{i}", [128, 2048], BF16) for i in range(2)]
    x1sl = [p.sb(f"x1sl{i}", [128, 512], BF16) for i in range(2)]
    xnf1 = p.sb("xnf1", [128, 2048], F32)
    xnb1 = [p.sb(f"xnb1_{i}", [128, 2048], BF16) for i in range(2)]
    q8a = p.sb("q8a", [128, 2048], I8)
    s1t = [p.sb(f"s1t{i}", [128, 512], BF16) for i in range(2)]
    gcol = p.sb("gcol", [128, 128], F32)
    scol = p.sb("scol", [128, 48], F32)
    ccol = p.sb("ccol", [128, 32], F32)
    crow = p.sb("crow", [1, 32], F32)
    onesf1 = p.sb("onesf1", [1, 128], F32)
    ones128f = p.sb("ones128f", [128, 1], F32)
    idbf = p.sb("s_idbf", [128, 128], BF16)
    idf32 = p.sb("s_idf32", [128, 128], F32)
    cpool = p.sb("cpool1", [128, 8], F32)
    lncol = p.sb("lncol1", [128, 16], F32)
    p.junkb = p.sb("junkb1", [128, 2048], BF16)

    PS = [p.ps(f"PT{b}", [128, 512], F32) for b in range(8)]
    PSbf = [ps.bitcast(BF16) for ps in PS]
    ln_eps = cpool[:, 0:1]

    p.dma("sync", idbf[:], io["idbf"][:], sem="init")
    tok_init = p.dma("sync", idf32[:], io["idf32"][:], sem="init")
    p.op("vector", "memset", cpool[:, 0:1], 1e-5)
    p.op("vector", "memset", onesf1[:], 1.0)
    t_ms = p.op("vector", "memset", ones128f[:], 1.0, done=True)

    p.dma("sync", ccol[:, 0:16], io["gsc"][:, 0:16], sem="init")
    tok_init = p.dma("sync", borow[:], io["wbias"][2:3, 0:2048], sem="init")
    tgs = tbo = tok_init
    tok_gb = [tgs, tgs, tgs]
    tok_bs = tbo
    tok_w8_done = tbo
    GAM = lambda mi: ccol[:, 8 + mi * 2: 9 + mi * 2]
    GR = lambda mi: ccol[:, 9 + mi * 2: 10 + mi * 2]

    tok_lnb = [None, None]
    tok_x1e = [None, None]
    tok_s1l = [None, None]
    tok_y = [None, None]
    tok_xqT_free = None
    tok_gv_free = tok_w8_done
    tok_wtb = [None, None]
    tok_bgv = [None, None]
    tok_q8a = None
    tok_xq3_free = None
    tok_first_p3 = ("PAIR", tok_w8_done, tok_bs)
    tmm2 = None
    tmm3 = None
    for g in range(NG):
        tg0 = g * TG
        t_xq = []
        t_xq_l = [None] * NTT
        scol_toks = [None] * NTT
        for tt in range(NTT):
            par = tt % 2
            row0 = tg0 + tt * 128
            tl = p.dma("sync", x1t[par][:], io["x1"][row0:row0 + 128, :],
                       after=[tok_lnb[par], tok_first_p3], sem=f"x1l{par}")
            _, tsq2, txn = emit_ln(p, None, xnf1[:], x1t[par][:], lncol,
                                   par * 8, ln_eps, [tl], want_f32=True)
            tok_lnb[par] = ("PAIR", txn, tsq2)
            trm = p.op("vector", "tensor_reduce", ccol[:, 16:17], xnf1[:],
                       AX.X, ALU.max, True, after=[txn], done=True)
            tcl = p.op("vector", "tensor_single_scalar", ccol[:, 17:18],
                       ccol[:, 16:17], 1e-5, ALU.max, after=[trm], done=True)
            trc = p.op("vector", "reciprocal", ccol[:, 18:19], ccol[:, 17:18],
                       after=[tcl], done=True)
            ts7 = p.op("vector", "tensor_single_scalar", ccol[:, 19:20],
                       ccol[:, 18:19], 127.0, ALU.mult, after=[trc], done=True)
            tf0 = p.op("vector", "tensor_scalar", ccol[:, 20:21],
                       ccol[:, 17:18], GAM(0), None, ALU.mult,
                       after=tmax(tcl, tok_gb[0]), done=True)
            tf0 = p.op("vector", "tensor_single_scalar", ccol[:, 22:23],
                       ccol[:, 20:21], 1.0 / 127.0, ALU.mult, after=[tf0],
                       done=True)
            tf1 = p.op("vector", "tensor_scalar", ccol[:, 21:22],
                       ccol[:, 17:18], GAM(1), None, ALU.mult,
                       after=tmax(tcl, tok_gb[1]), done=True)
            tf1 = p.op("vector", "tensor_single_scalar", ccol[:, 23:24],
                       ccol[:, 21:22], 1.0 / 127.0, ALU.mult, after=[tf1],
                       done=True)
            c0 = tt * 12
            tcp = p.op("vector", "tensor_copy", scol[:, c0:c0 + 1],
                       ccol[:, 19:20], after=[ts7], done=True)
            tf0c = p.op("vector", "tensor_copy", scol[:, c0 + 1:c0 + 2],
                        ccol[:, 22:23], after=[tf0], done=True)
            tf1c = p.op("vector", "tensor_copy", scol[:, c0 + 2:c0 + 3],
                        ccol[:, 23:24], after=[tf1], done=True)
            scol_toks[tt] = ("PAIR", tcp, tf0c, tf1c)
            t8 = p.op("vector", "tensor_scalar", q8a[:], xnf1[:],
                      ccol[:, 19:20], None, ALU.mult,
                      after=tmax(ts7, tok_q8a), done=True)
            tqb = p.op("vector", "tensor_copy", xnb1[par][:], q8a[:],
                       after=[t8], done=True)
            tok_q8a = tqb
            ttp = p.op("tensor", "transpose", PS[3][0:1, 0:128],
                       scol[:, c0:c0 + 1], idf32[:],
                       after=tmax(tcp, tok_init, p.bank[3], tok_xqT_free),
                       done=True)
            trw = p.op("vector", "tensor_copy",
                       xqrow[0:1, tt * 128:(tt + 1) * 128],
                       PS[3][0:1, 0:128], after=[ttp], done=True)
            p.bank[3] = trw
            t_xq.append(trw)
            t_xq_tt = trw
            for fc in range(16):
                bank = 2 if fc % 2 == 0 else 6
                tm = p.op("tensor", "transpose", PSbf[bank][:, 0:128],
                          xnb1[par][:, fc * 128:(fc + 1) * 128], idbf[:],
                          after=tmax(tqb, tok_init, p.bank[bank], tok_xqT_free),
                          done=True)
                tev = p.op("vector", "tensor_copy",
                           xqT[:, fc * 512 + tt * 128: fc * 512 + (tt + 1) * 128],
                           PSbf[bank][:, 0:128], after=[tm], done=True)
                p.bank[bank] = tev
                t_xq.append(tev)
            t_xq_l[tt] = ("PAIR", tev, t_xq_tt)
        tok_xq = t_xq[-1]

        tok_gv = None
        for hc in range(16):
            par = 0
            twl = p.dma("gpsimd", wtb[par].rearrange("p (c o) -> p c o", c=16),
                        io["wgi8"][:, hc * 512:(hc + 1) * 512]
                        .rearrange("(c p) o -> p c o", p=128),
                        after=[tok_wtb[par], tok_w8_done], sem=f"wt{par}")
            tbg_ = p.dma("sync", bgv[0:1, par * 1024: par * 1024 + 512],
                         io["wbias"][0:1, hc * 512:(hc + 1) * 512],
                         after=[tok_bgv[par], tok_bs], sem=f"bg{par}")
            for tt in range(NTT):
                bank = tt % 2
                for fc in range(16):
                    p.op("tensor", "matmul", PS[bank][:, 0:512],
                         xqT[:, fc * 512 + tt * 128: fc * 512 + (tt + 1) * 128],
                         wtb[par][:, fc * 512:(fc + 1) * 512],
                         start=(fc == 0), stop=False,
                         after=tmax(twl, t_xq_l[tt], p.bank[bank]) if fc == 0
                         else None)
                p.op("tensor", "matmul", PS[bank][:, 0:512],
                     xqrow[0:1, tt * 128:(tt + 1) * 128],
                     bgv[0:1, par * 1024: par * 1024 + 512],
                     start=False, stop=True,
                     after=tmax(tbg_, t_xq_l[tt]),
                     done=True)
                tmm = ("eT", p.cnt["eT"])
                tge = p.op("scalar", "activation",
                           gsb[:, tt * 512:(tt + 1) * 512],
                           PS[bank][:, 0:512], AF.Silu,
                           scale=scol[:, tt * 12 + 1:tt * 12 + 2],
                           after=tmax(tmm, tok_gv, scol_toks[tt]), done=True)
                p.bank[bank] = tge
            tok_wtb[0] = tmm
            tok_bgv[0] = tmm
            par = 1
            twl2 = p.dma("gpsimd", wtb[par].rearrange("p (c o) -> p c o", c=16),
                         io["wvi8"][:, hc * 512:(hc + 1) * 512]
                         .rearrange("(c p) o -> p c o", p=128),
                         after=[tok_wtb[par], tok_bgv[par]], sem=f"wt{par}")
            tbv_ = p.dma("sync", bgv[0:1, par * 1024 + 512: par * 1024 + 1024],
                         io["wbias"][1:2, hc * 512:(hc + 1) * 512],
                         after=[tok_bgv[par], tok_bs], sem=f"bv{par}")
            for tt in range(NTT):
                bank = tt % 2
                for fc in range(16):
                    p.op("tensor", "matmul", PS[bank][:, 0:512],
                         xqT[:, fc * 512 + tt * 128: fc * 512 + (tt + 1) * 128],
                         wtb[par][:, fc * 512:(fc + 1) * 512],
                         start=(fc == 0), stop=False,
                         after=tmax(twl2, p.bank[bank]) if fc == 0 else None)
                p.op("tensor", "matmul", PS[bank][:, 0:512],
                     xqrow[0:1, tt * 128:(tt + 1) * 128],
                     bgv[0:1, par * 1024 + 512: par * 1024 + 1024],
                     start=False, stop=True,
                     after=tmax(tbv_) if tt == 0 else None,
                     done=True)
                tmm2 = ("eT", p.cnt["eT"])
                tve = p.op("scalar", "activation",
                           vsb2[:, tt * 512:(tt + 1) * 512],
                           PS[bank][:, 0:512], AF.Copy,
                           after=tmax(tmm2, tok_gv), done=True)
                p.bank[bank] = tve
                tok_gv = p.op("vector", "tensor_tensor",
                              gvsb[:, tt * 8192 + hc * 512:
                                   tt * 8192 + (hc + 1) * 512],
                              gsb[:, tt * 512:(tt + 1) * 512],
                              vsb2[:, tt * 512:(tt + 1) * 512], ALU.mult,
                              after=tmax(tge, tve, tok_gv_free), done=True)
            tok_wtb[par] = tmm2
            tok_bgv[par] = tmm2
        tok_gv_done = tok_gv

        t_rq = [None] * NTT
        trw_l = [None] * NTT
        tsl3_l = [None] * NTT
        for tt in range(NTT):
            c0 = tt * 12
            trm = p.op("vector", "tensor_reduce", ccol[:, 16:17],
                       gvsb[:, tt * 8192:(tt + 1) * 8192], AX.X, ALU.max, True,
                       after=[tok_gv_done], done=True)
            tgg = p.op("vector", "tensor_scalar", ccol[:, 17:18],
                       ccol[:, 16:17], scol[:, c0 + 2:c0 + 3], None, ALU.mult,
                       after=tmax(trm, scol_toks[tt]), done=True)
            tgc = p.op("vector", "tensor_single_scalar", ccol[:, 18:19],
                       ccol[:, 17:18], 1e-5, ALU.max, after=[tgg], done=True)
            trc = p.op("vector", "reciprocal", ccol[:, 19:20], ccol[:, 18:19],
                       after=[tgc], done=True)
            t1_ = p.op("vector", "tensor_single_scalar", ccol[:, 20:21],
                       ccol[:, 19:20], 127.0, ALU.mult, after=[trc], done=True)
            ts3 = p.op("vector", "tensor_scalar", ccol[:, 21:22],
                       ccol[:, 20:21], scol[:, c0 + 2:c0 + 3], None, ALU.mult,
                       after=[t1_], done=True)
            tg3 = p.op("vector", "tensor_scalar", ccol[:, 22:23],
                       ccol[:, 18:19], GAM(2), None, ALU.mult,
                       after=tmax(tgc, tok_gb[2]), done=True)
            tg3 = p.op("vector", "tensor_single_scalar", ccol[:, 23:24],
                       ccol[:, 22:23], 1.0 / 127.0, ALU.mult, after=[tg3],
                       done=True)
            tsl3 = p.op("vector", "tensor_copy", scol[:, c0 + 4:c0 + 5],
                        ccol[:, 23:24], after=[tg3], done=True)
            tsl3_l[tt] = tsl3
            tcp = p.op("vector", "tensor_copy", scol[:, c0 + 3:c0 + 4],
                       ccol[:, 20:21], after=[t1_], done=True)
            ttp = p.op("tensor", "transpose", PS[3][0:1, 0:128],
                       scol[:, c0 + 3:c0 + 4], idf32[:],
                       after=tmax(tcp, p.bank[3]), done=True)
            trw = p.op("vector", "tensor_copy",
                       r3row[0:1, tt * 128:(tt + 1) * 128],
                       PS[3][0:1, 0:128], after=[ttp], done=True)
            p.bank[3] = trw
            trw_l[tt] = trw
            t8 = p.op("vector", "tensor_scalar", q83[:],
                      gvsb[:, tt * 8192:(tt + 1) * 8192], ccol[:, 21:22], None,
                      ALU.mult,
                      after=tmax(ts3, tok_xq3_free,
                                 t_rq[tt - 1] if tt > 0 else None), done=True)
            tqb = p.op("vector", "tensor_copy", xq3b[:], q83[:], after=[t8],
                       done=True)
            for hcc in range(64):
                bank = 2 if hcc % 2 == 0 else 6
                tm = p.op("tensor", "transpose", PSbf[bank][:, 0:128],
                          xq3b[:, hcc * 128:(hcc + 1) * 128], idbf[:],
                          after=tmax(tqb, p.bank[bank]), done=True)
                tev = p.op("vector", "tensor_copy",
                           gvsb[:, tt * 8192 + hcc * 128:
                                tt * 8192 + (hcc + 1) * 128],
                           PSbf[bank][:, 0:128], after=[tm], done=True)
                p.bank[bank] = tev
            t_rq[tt] = tev

        OB = [0, 1, 4, 5]
        tok_lastmm = None
        tok_g1e_war = None
        for oc in range(4):
            for qb in range(4):
                parw = qb % 2
                twl = p.dma("gpsimd",
                            wtb[parw].rearrange("p (c o) -> p c o", c=16),
                            io["woi8"][qb * 2048:(qb + 1) * 2048,
                                       oc * 512:(oc + 1) * 512]
                            .rearrange("(c p) o -> p c o", p=128),
                            after=[tok_wtb[parw], tok_w8_done],
                            sem=f"wt{parw}")
                for tt in range(NTT):
                    bank = OB[tt]
                    for j in range(16):
                        hcc = qb * 16 + j
                        p.op("tensor", "matmul", PS[bank][:, 0:512],
                             gvsb[:, tt * 8192 + hcc * 128:
                                  tt * 8192 + (hcc + 1) * 128],
                             wtb[parw][:, j * 512:(j + 1) * 512],
                             start=(qb == 0 and j == 0), stop=False,
                             after=tmax(twl, t_rq[tt],
                                        p.bank[bank] if qb == 0 else None)
                             if j == 0 else None, done=(j == 15))
                tok_wtb[parw] = ("eT", p.cnt["eT"])
            for tt in range(NTT):
                bank = OB[tt]
                row0 = tg0 + tt * 128
                p.op("tensor", "matmul", PS[bank][:, 0:512],
                     r3row[0:1, tt * 128:(tt + 1) * 128],
                     borow[0:1, oc * 512:(oc + 1) * 512],
                     start=False, stop=True,
                     after=tmax(tbo, trw_l[tt]), done=True)
                tmm3 = ("eT", p.cnt["eT"])
                tok_lastmm = tmm3
                tsil = p.op("scalar", "activation", g1e[:], PS[bank][:, 0:512],
                            AF.Silu, scale=scol[:, tt * 12 + 4:tt * 12 + 5],
                            after=tmax(tmm3, tsl3_l[tt], tok_g1e_war),
                            done=True)
                p.bank[bank] = tsil
                pare = tt % 2
                tsl = p.dma("sync", s1t[pare][:],
                            io["s1"][row0:row0 + 128, oc * 512:(oc + 1) * 512],
                            after=[tok_s1l[pare]], sem=f"s1l{pare}")
                txl = p.dma("sync", x1sl[pare][:],
                            io["x1"][row0:row0 + 128, oc * 512:(oc + 1) * 512],
                            after=[tok_x1e[pare]], sem=f"x1e{pare}")
                tsg_ = p.op("vector", "tensor_tensor", sg1[:], s1t[pare][:],
                            g1e[:], ALU.mult, after=tmax(tsl, tsil), done=True)
                tok_s1l[pare] = tsg_
                tok_g1e_war = tsg_
                ty_ = p.op("vector", "tensor_tensor", yt[pare][:], sg1[:],
                           x1sl[pare][:], ALU.add,
                           after=tmax(tsg_, txl, tok_y[pare]), done=True)
                tok_x1e[pare] = ty_
                tok_y[pare] = p.dma(
                    "sync", io["y"][row0:row0 + 128, oc * 512:(oc + 1) * 512],
                    yt[pare][:], after=[ty_], sem=f"ys{pare}")
        tok_xq3_free = tok_lastmm
        tok_xqT_free = tmm2
        tok_gv_free = tok_lastmm

    p.wait("sync", [tok_y[0], tok_y[1]])
    return tok_y


import concourse.bass as _bass
from concourse.bass_utils import run_bass_kernel_spmd as _run_spmd

_CACHE = {}


def _build_fused_nc():
    nc = _bass.Bass("TRN2", target_bir_lowering=False)
    io = {}
    io["x0"] = nc.dram_tensor("x0", [2048, 2048], F32, kind="ExternalInput")
    io["qkvw_t"] = nc.dram_tensor("qkvw_t", [2048, 6144], F32, kind="ExternalInput")
    io["gatew_t"] = nc.dram_tensor("gatew_t", [2048, 2048], F32, kind="ExternalInput")
    io["outw_t"] = nc.dram_tensor("outw_t", [2048, 2048], F32, kind="ExternalInput")
    io["normw"] = nc.dram_tensor("normw", [1, 2048], F32, kind="ExternalInput")
    io["ffn0w"] = nc.dram_tensor("ffn0w", [2048, 2048], F32, kind="ExternalInput")
    io["ddt"] = nc.dram_tensor("d_ddt", [128, 16384], BF16, kind="ExternalInput")
    io["qdkd"] = nc.dram_tensor("d_qdkd", [128, 128], F32, kind="ExternalInput")
    io["idbf"] = nc.dram_tensor("d_idbf", [128, 128], BF16, kind="ExternalInput")
    io["idf32"] = nc.dram_tensor("d_idf32", [128, 128], F32, kind="ExternalInput")
    io["x1"] = nc.dram_tensor("x1", [2048, 2048], BF16, kind="Internal")
    io["wdeq_scr"] = nc.dram_tensor("wdeq_scr", [2048, 2048], BF16, kind="Internal")
    io["wq"] = nc.dram_tensor("wq", [2048, 2048], F32, kind="ExternalInput")
    io["wk"] = nc.dram_tensor("wk", [2048, 128], F32, kind="ExternalInput")
    io["wv"] = nc.dram_tensor("wv", [2048, 128], F32, kind="ExternalInput")
    for nm, shape, dt_ in [("pswap", [128, 128], BF16),
                           ("rcq", [128, 2048], BF16), ("rsq", [128, 2048], BF16),
                           ("rck", [128, 2048], BF16), ("rsk", [128, 2048], BF16),
                           ("tband", [128, 1024], BF16)]:
        io[nm] = nc.dram_tensor("d_" + nm, shape, dt_, kind="ExternalInput")
    io["s1"] = nc.dram_tensor("s1", [2048, 2048], BF16, kind="Internal")
    io["gwt"] = nc.dram_tensor("gwt", [2048, 8192], F32, kind="ExternalInput")
    io["vwt"] = nc.dram_tensor("vwt", [2048, 8192], F32, kind="ExternalInput")
    io["owt"] = nc.dram_tensor("owt", [8192, 2048], F32, kind="ExternalInput")
    io["gb"] = nc.dram_tensor("gb", [1, 8192], F32, kind="ExternalInput")
    io["vb"] = nc.dram_tensor("vb", [1, 8192], F32, kind="ExternalInput")
    io["ob"] = nc.dram_tensor("ob", [1, 2048], F32, kind="ExternalInput")
    io["wgi8"] = nc.dram_tensor("wgi8", [2048, 8192], I8, kind="Internal")
    io["wvi8"] = nc.dram_tensor("wvi8", [2048, 8192], I8, kind="Internal")
    io["woi8"] = nc.dram_tensor("woi8", [8192, 2048], I8, kind="Internal")
    io["wbias"] = nc.dram_tensor("wbias", [4, 8192], BF16, kind="Internal")
    io["gsc"] = nc.dram_tensor("gsc", [128, 16], F32, kind="Internal")
    io["y"] = nc.dram_tensor("y", [2048, 2048], F32, kind="ExternalOutput")

    es = contextlib.ExitStack()
    with es:
        p = Prog(nc, es)
        p.push_scope()
        p.prefix = "A_"
        build_l0(p, io)
        p.barrier()
        p.pop_scope()
        p.push_scope()
        p.prefix = "B_"
        build_l1a(p, io)
        p.barrier()
        p.pop_scope()
        p.push_scope()
        p.prefix = "C_"
        build_l1b(p, io)
        p.pop_scope()
        p.finalize()
    return nc


def kernel(L0_attn_hidden_states, L0_attn_qkv_w, L0_attn_out_w, L0_attn_gate_w,
           L0_attn_norm_w, L0_ffn_weight, L1_attn_Wq, L1_attn_Wk, L1_attn_Wv,
           L1_ffn_gate_w, L1_ffn_gate_b, L1_ffn_val_w, L1_ffn_val_b,
           L1_ffn_out_w, L1_ffn_out_b):
    C = host_consts()
    if "nc" not in _CACHE:
        _CACHE["nc"] = _build_fused_nc()

    x = np.ascontiguousarray(L0_attn_hidden_states.astype(np.float32, copy=False))
    shared = dict(
        qkvw_t=np.ascontiguousarray(L0_attn_qkv_w.T),
        gatew_t=np.ascontiguousarray(L0_attn_gate_w.T),
        outw_t=np.ascontiguousarray(L0_attn_out_w.T),
        normw=np.ascontiguousarray(L0_attn_norm_w.reshape(1, 2048)),
        ffn0w=np.ascontiguousarray(L0_ffn_weight),
        d_ddt=C["ddt"], d_qdkd=C["qdkd"], d_idbf=C["idbf"], d_idf32=C["idf32"],
        wq=np.ascontiguousarray(L1_attn_Wq), wk=np.ascontiguousarray(L1_attn_Wk),
        wv=np.ascontiguousarray(L1_attn_Wv),
        d_pswap=C["pswap"], d_rcq=C["rcq"], d_rsq=C["rsq"], d_rck=C["rck"],
        d_rsk=C["rsk"], d_tband=C["tband"],
        gwt=np.ascontiguousarray(L1_ffn_gate_w.T),
        vwt=np.ascontiguousarray(L1_ffn_val_w.T),
        owt=np.ascontiguousarray(L1_ffn_out_w.T),
        gb=np.ascontiguousarray(L1_ffn_gate_b.reshape(1, -1)),
        vb=np.ascontiguousarray(L1_ffn_val_b.reshape(1, -1)),
        ob=np.ascontiguousarray(L1_ffn_out_b.reshape(1, -1)))
    maps = [dict(shared, x0=np.ascontiguousarray(x[b])) for b in range(8)]
    res = _run_spmd(_CACHE["nc"], maps, core_ids=list(range(8)))
    return np.stack([np.asarray(r["y"], dtype=np.float32) for r in res.results])


# version 30
# speedup vs baseline: 1.0115x; 1.0115x over previous
import numpy as np
import ml_dtypes
import contextlib
import concourse.bass as bass
import concourse.mybir as mybir

F32 = mybir.dt.float32
BF16 = mybir.dt.bfloat16
FP8 = mybir.dt.float8e4
I8 = mybir.dt.int8
AF = mybir.ActivationFunctionType
ALU = mybir.AluOpType
AX = mybir.AxisListType

ENGINES = ("tensor", "vector", "scalar", "gpsimd", "sync")


def host_consts():
    H, D, BS = 32, 64, 256
    base = 1.0 / 2.0 ** (8.0 / H)
    rate = base ** (np.arange(H, dtype=np.float64) + 1.0)
    factor = 1.0 - 0.0 / (32 - 1 + 1e-05) + 1e-05
    slope = (rate * factor).astype(np.float64)
    r = np.arange(BS, dtype=np.float64) + 1.0
    qd = np.exp(-slope[:, None] * r[None, :])
    kd = np.exp(-slope[:, None] * (BS - r[None, :]))
    bd = np.exp(-slope * BS)
    i = np.arange(BS)
    dd = i[None, :] - i[:, None]
    ddT = np.where(dd >= 0, np.exp(-slope[:, None, None] * dd[None]), 0.0)
    ddT_sb = np.zeros((128, 32 * 2 * 256), np.float32)
    for h in range(H):
        for jc in range(2):
            ddT_sb[:, (h * 2 + jc) * 256:(h * 2 + jc + 1) * 256] = \
                ddT[h, jc * 128:(jc + 1) * 128, :]
    ddT_sb = ddT_sb.astype(ml_dtypes.bfloat16)
    qdkd = np.zeros((128, 128), np.float32)
    for h in range(H):
        for c in range(2):
            qdkd[:, h * 2 + c] = qd[h, c * 128:(c + 1) * 128]
            qdkd[:, 64 + h * 2 + c] = kd[h, c * 128:(c + 1) * 128]
    idbf = np.eye(128).astype(ml_dtypes.bfloat16)
    idf32 = np.eye(128).astype(np.float32)

    HD, TH, L = 128, 10000.0, 2048
    inv_freq = 1.0 / TH ** (np.arange(0, HD, 2, dtype=np.float64) / HD)
    t = np.arange(L, dtype=np.float64)
    f = np.repeat(np.outer(t, inv_freq), 2, axis=-1)
    cosT = np.cos(f).T
    sinT = np.sin(f).T
    sgn = np.where(np.arange(HD) % 2 == 0, -1.0, 1.0)[:, None]
    sinS = sinT * sgn
    sc = 1.0 / np.sqrt(128.0)
    k = np.arange(128)[:, None]
    m = np.arange(1024)[None, :]
    tband = (k <= m - 384).astype(ml_dtypes.bfloat16)
    P = np.zeros((128, 128), np.float32)
    for a in range(128):
        partner = a + 1 if a % 2 == 0 else a - 1
        P[partner, a] = 1.0
    return dict(ddt=np.ascontiguousarray(ddT_sb), qdkd=qdkd, idbf=idbf, idf32=idf32,
                bd=bd, pswap=P.astype(ml_dtypes.bfloat16), tband=tband,
                rcq=(cosT * sc).astype(ml_dtypes.bfloat16),
                rsq=(sinS * sc).astype(ml_dtypes.bfloat16),
                rck=cosT.astype(ml_dtypes.bfloat16),
                rsk=sinS.astype(ml_dtypes.bfloat16))


class Prog:
    def __init__(self, nc, es):
        self.nc = nc
        self.es = es
        self.q = {e: [] for e in ENGINES}
        self.cnt = {}
        self.sems = {}
        for name in ("eT", "eV", "eS"):
            h = es.enter_context(nc.semaphore(name))
            self.sems[name] = h
            self.cnt[name] = 0
        self.bank = {b: None for b in range(8)}

    def dsem(self, name):
        if name not in self.sems:
            self.sems[name] = self.es.enter_context(self.nc.semaphore(name))
            self.cnt[name] = 0
        return name

    def sb(self, name, shape, dtype):
        es = self.scopes[-1] if getattr(self, "scopes", None) else self.es
        name = getattr(self, "prefix", "") + name
        return es.enter_context(self.nc.sbuf_tensor(name, shape, dtype))

    def ps(self, name, shape, dtype=F32):
        es = self.scopes[-1] if getattr(self, "scopes", None) else self.es
        name = getattr(self, "prefix", "") + name
        return es.enter_context(self.nc.psum_tensor(name, shape, dtype))

    def push_scope(self):
        if not hasattr(self, "scopes"):
            self.scopes = []
        self.scopes.append(contextlib.ExitStack())

    def pop_scope(self):
        self.scopes.pop().close()

    def barrier(self):
        toks = [(s, self.cnt[s]) for s in self.cnt]
        for eng in ENGINES:
            self.wait(eng, toks)
        self.bank = {b: None for b in range(8)}

    def _waits(self, after):
        w = {}
        def add(t):
            if t is None:
                return
            if t[0] == "PAIR":
                for u in t[1:]:
                    add(u)
                return
            s, v = t
            w[s] = max(w.get(s, 0), v)
        for t in (after or []):
            add(t)
        return sorted(w.items())

    def _emit(self, eng, meth, args, kw, waits, incs):
        def run(e, meth=meth, args=args, kw=kw, waits=waits, incs=incs):
            for s, v in waits:
                e.wait_ge(self.sems[s], v)
            r = getattr(e, meth)(*args, **kw)
            for s, n in incs:
                r = r.then_inc(self.sems[s], n)
        self.q[eng].append(run)

    def op(self, eng, meth, *args, after=None, done=False, **kw):
        esem = {"tensor": "eT", "vector": "eV", "scalar": "eS"}.get(eng)
        incs = []
        tok = None
        if done:
            self.cnt[esem] += 1
            incs = [(esem, 1)]
            tok = (esem, self.cnt[esem])
        self._emit(eng, meth, args, kw, self._waits(after), incs)
        return tok

    def dma(self, queue, out, in_, after=None, sem="misc", **kw):
        qsem = self.dsem(sem)
        self.cnt[qsem] += 16
        self._emit(queue, "dma_start", (), dict(out=out, in_=in_, **kw),
                   self._waits(after), [(qsem, 16)])
        return (qsem, self.cnt[qsem])

    def dma_tp(self, out, in_, after=None, sem="misc"):
        qsem = self.dsem(sem)
        self.cnt[qsem] += 16
        self._emit("sync", "dma_start_transpose", (out, in_),
                   {}, self._waits(after), [(qsem, 16)])
        return (qsem, self.cnt[qsem])

    def wait(self, eng, tokens):
        for s, v in self._waits(tokens):
            def run(e, s=s, v=v):
                e.wait_ge(self.sems[s], v)
            self.q[eng].append(run)

    def finalize(self):
        block = self.es.enter_context(self.nc.Block())
        for eng in ENGINES:
            ops = self.q[eng]
            dec = getattr(block, eng)
            def make(ops):
                def body(e):
                    for f in ops:
                        f(e)
                return body
            dec(make(ops))


def tmax(*tokens):
    out = {}
    def add(t):
        if t is None:
            return
        if t[0] == "PAIR":
            for u in t[1:]:
                add(u)
            return
        s, v = t
        out[s] = max(out.get(s, 0), v)
    for t in tokens:
        add(t)
    return [(s, v) for s, v in out.items()]


def emit_ln(p, dst, dst2, x_ap, lncol, c0, eps_ap, after, want_f32=False):
    tsum = p.op("vector", "tensor_reduce", lncol[:, c0:c0 + 1], x_ap,
                AX.X, ALU.add, after=after, done=True)
    tsq = p.op("scalar", "activation", p.junkb[:], x_ap, AF.Square,
               accum_out=lncol[:, c0 + 1:c0 + 2], after=after, done=True)
    tnm = p.op("vector", "tensor_single_scalar", lncol[:, c0 + 2:c0 + 3],
               lncol[:, c0:c0 + 1], -1.0 / 2048.0, ALU.mult, after=[tsum],
               done=True)
    tms = p.op("vector", "tensor_scalar", lncol[:, c0 + 3:c0 + 4],
               lncol[:, c0 + 2:c0 + 3], lncol[:, c0 + 2:c0 + 3], None, ALU.mult,
               after=[tnm], done=True)
    p.op("vector", "tensor_single_scalar", lncol[:, c0 + 4:c0 + 5],
         lncol[:, c0 + 1:c0 + 2], 1.0 / 2048.0, ALU.mult, after=[tsq])
    tvar = p.op("vector", "tensor_scalar", lncol[:, c0 + 5:c0 + 6],
                lncol[:, c0 + 4:c0 + 5], lncol[:, c0 + 3:c0 + 4], None,
                ALU.subtract, after=[tms], done=True)
    tsd = p.op("scalar", "activation", lncol[:, c0 + 6:c0 + 7],
               lncol[:, c0 + 5:c0 + 6], AF.Sqrt, bias=eps_ap, after=[tvar],
               done=True)
    trs = p.op("vector", "reciprocal", lncol[:, c0 + 4:c0 + 5],
               lncol[:, c0 + 6:c0 + 7], after=[tsd], done=True)
    tnr = p.op("vector", "tensor_scalar", lncol[:, c0 + 3:c0 + 4],
               lncol[:, c0 + 2:c0 + 3], lncol[:, c0 + 4:c0 + 5], None, ALU.mult,
               after=[trs], done=True)
    rstd = lncol[:, c0 + 4:c0 + 5]
    nmr = lncol[:, c0 + 3:c0 + 4]
    tok_f32 = None
    if dst2 is not None:
        tok_f32 = p.op("vector", "tensor_scalar", dst2, x_ap,
                       rstd, nmr, ALU.mult, ALU.add, after=[tnr], done=True)
    tok = None
    if dst is not None:
        tok = p.op("vector", "tensor_scalar", dst, x_ap,
                   rstd, nmr, ALU.mult, ALU.add, after=[tnr], done=True)
    return tok, tsq, tok_f32


def build_l0(p, io, MB=512, dbg=None):
    dbg = dbg or {}
    nc = p.nc
    NMB = 2048 // MB
    NT = MB // 128
    NBLK = MB // 256
    NOC = 4
    H = 32
    HPG = 4
    NHG = H // HPG

    bd = host_consts()["bd"]

    ddhg = p.sb("ddhg", [128, HPG * 2 * 256], BF16)
    qdkd = p.sb("qdkd", [128, 128], F32)
    idbf = p.sb("idbf", [128, 128], BF16)
    nwb = p.sb("nwb", [128, 2048], BF16)
    nwrow_b = p.sb("nwrow_b", [1, 2048], BF16)
    onesb = p.sb("onesb", [1, 128], BF16)
    cpool = p.sb("cpool", [128, 8], F32)
    state = p.sb("state", [64, 2048], F32)
    stbf = p.sb("stbf", [64, 2048], BF16)

    xn0T = p.sb("xn0T", [128, 16 * MB], BF16)
    attn = p.sb("attn", [128, NT * 2048], BF16)
    ga = p.sb("ga", [128, NT * 2048], BF16)
    xdqT = xn0T
    gaT = attn

    x0t = [p.sb(f"x0t{i}", [128, 2048], F32) for i in range(2)]
    xnf = p.sb("xnf", [128, 2048], F32)
    xnb = [p.sb(f"xnb{i}", [128, 2048], BF16) for i in range(2)]
    lncol = p.sb("lncol", [128, 16], F32)
    p.junkb = p.sb("junkb", [128, 2048], BF16)
    q8 = [p.sb(f"q8_{i}", [128, 2048], FP8) for i in range(2)]
    amcol = p.sb("amcol", [128, 128], F32)
    rstdc = p.sb("rstdc", [128, 8], F32)

    whead = [p.sb(f"whead{i}", [128, 2048], BF16) for i in range(2)]
    wkv = p.sb("wkv", [128, 16 * 512], BF16)
    qf = p.sb("qf", [64, HPG * MB], BF16)
    kf = p.sb("kf", [64, HPG * MB], BF16)
    kvt = p.sb("kvt", [128, NT * 512], BF16)
    awb = p.sb("awb", [128, 1024], BF16)
    tin = p.sb("tin", [128, 128], F32)
    ktkd = p.sb("ktkd", [128, 128], BF16)

    wstkA = p.sb("wstkA", [128, 16 * 512], BF16)
    wstkF = p.sb("wstkF", [128, 16 * 512], BF16)
    gateb = p.sb("gateb", [128, 512], BF16)
    nwg = p.sb("nwg", [128, 512], BF16)
    s0e = p.sb("s0e", [128, 512], F32)
    g0e = p.sb("g0e", [128, 512], F32)
    sge = p.sb("sge", [128, 512], F32)
    x1b = [p.sb(f"x1b{i}", [128, 512], BF16) for i in range(2)]
    x0sl = [p.sb(f"x0sl{i}", [128, 512], F32) for i in range(2)]

    PS = [p.ps(f"PS{b}", [128, 512], F32) for b in range(8)]
    PSbf = [ps.bitcast(BF16) for ps in PS]

    p.dma("sync", qdkd[:], io["qdkd"][:], sem="init")
    p.dma("sync", idbf[:], io["idbf"][:], sem="init")
    t_init_all = p.dma("gpsimd", nwrow_b[:], io["normw"][:], sem="init")
    t_qdkd = t_id = t_nw = t_init_all
    p.op("vector", "memset", cpool[:, 0:1], 1e-5)
    p.op("vector", "memset", cpool[:, 1:2], 1.0 / 2048.0)
    p.op("vector", "memset", state[:], 0.0)
    _t_st0 = p.op("vector", "memset", stbf[:], 0.0, done=True)
    tok_st_h = [_t_st0] * H
    t_ones = p.op("vector", "memset", onesb[:], 1.0, done=True)
    t_nwb = None
    for ocx in range(4):
        tm = p.op("tensor", "matmul", PS[3][:, 0:512], onesb[:],
                  nwrow_b[0:1, ocx * 512:(ocx + 1) * 512],
                  start=True, stop=True, after=tmax(t_ones, t_init_all, p.bank[3]),
                  done=True)
        t_nwb = p.op("vector", "tensor_copy", nwb[:, ocx * 512:(ocx + 1) * 512],
                     PS[3][:, 0:512], after=[tm], done=True)
        p.bank[3] = t_nwb

    ln_eps = cpool[:, 0:1]
    inv2048 = cpool[:, 1:2]

    sg = None
    if "gwt" in io:
        wsg = p.sb("wsg", [128, 2048], F32)
        gcl = p.sb("gcl", [128, 80], F32)
        gcc = p.sb("gcc", [128, 16], F32)
        gcr = p.sb("gcr", [1, 32], F32)
        onesf1L = p.sb("onesf1L", [1, 128], F32)
        ones128fL = p.sb("ones128fL", [128, 1], F32)
        p.op("vector", "memset", onesf1L[:], 1.0)
        t_msL = p.op("vector", "memset", ones128fL[:], 1.0, done=True)
        sgbufs = dict(wsg=wsg, gcl=gcl, gcc=gcc, gcr=gcr, onesf1=onesf1L,
                      ones128f=ones128fL, t_ms=t_msL, PSG=PS[3], gbank=3)
        sg = gamma_gen_l0(p, io, sgbufs)

    def dr(n=1):
        if sg is not None:
            for _ in range(n):
                next(sg)

    tok_wdq_store = [None, None]
    tok_q8_reader = None
    t8 = None
    tdq = None
    for rc in range(16):
        par = rc % 2
        tl = p.dma("sync", x0t[par][:], io["ffn0w"][rc * 128:(rc + 1) * 128, :],
                   after=[tok_wdq_store[par]], sem=f"x0{par}")
        am = amcol[:, 0:16]
        rc_ = amcol[:, 16:32]
        ds = amcol[:, 32:48]
        sc = amcol[:, 48:64]
        tred = p.op("vector", "tensor_reduce", am,
                    x0t[par][:].rearrange("p (b i) -> p b i", i=128),
                    AX.X, ALU.max, True, after=tmax(tl, tok_q8_reader),
                    done=True)
        trcp = p.op("vector", "reciprocal", rc_, am, after=[tred], done=True)
        tscm = p.op("vector", "tensor_single_scalar", sc, rc_, 224.0, ALU.mult,
                    after=[trcp], done=True)
        p.op("vector", "tensor_single_scalar", ds, am, 1.0 / 224.0, ALU.mult,
             after=[tred])
        for b in range(16):
            t8 = p.op("vector", "tensor_scalar", q8[0][:, b * 128:(b + 1) * 128],
                      x0t[par][:, b * 128:(b + 1) * 128], sc[:, b:b + 1],
                      None, ALU.mult,
                      after=[tok_q8_reader, tscm] if b == 0 else None,
                      done=(b == 15))
        for b in range(16):
            tdq = p.op("scalar", "activation", xnb[par][:, b * 128:(b + 1) * 128],
                       q8[0][:, b * 128:(b + 1) * 128], AF.Copy,
                       scale=ds[:, b:b + 1],
                       after=tmax(t8, tok_wdq_store[par]) if b == 0 else None,
                       done=(b == 15))
        tok_q8_reader = tdq
        tok_wdq_store[par] = p.dma(
            "sync", io["wdeq_scr"][rc * 128:(rc + 1) * 128, :], xnb[par][:],
            after=[tdq], sem=f"c0s{par}")
        dr()
    t_wdq_done = ("PAIR", tok_wdq_store[0], tok_wdq_store[1])

    tok_xn0T_free = None
    tok_attn_free = None
    tok_ga_free = None
    tok_qk_war = [None] * HPG
    tok_lnbuf = [("PAIR", t8, tdq, tok_wdq_store[0]),
                 ("PAIR", t8, tdq, tok_wdq_store[1])]
    tok_cbuf = [tdq, None]
    tok_vt_war = None
    tok_tin_war = [None, None]
    tok_awb_dump = [None]
    tok_kvt_dve_war = None
    tok_ktkd_war = None
    tok_whead = [None, None]
    tok_wstkA = None
    tok_ddhg = None
    tok_wf = None
    tok_x1buf = [None, None]
    tsg = None
    last_x1_store = None

    for mb in range(NMB):
        tokbase = mb * MB

        tA = []
        for tt in range(NT):
            par = tt % 2
            row0 = tokbase + tt * 128
            tl = p.dma("sync", x0t[par][:], io["x0"][row0:row0 + 128, :],
                       after=[tok_lnbuf[par]], sem=f"x0{par}")
            tyb, tsq, _ = emit_ln(p, xnb[par][:], None, x0t[par][:], lncol,
                                  par * 8, ln_eps, [tl])
            tok_lnbuf[par] = ("PAIR", tyb, tsq)
            dr()
            for fc in range(16):
                bank = 2 + (fc % 2)
                tm = p.op("tensor", "transpose", PSbf[bank][:, 0:128],
                          xnb[par][:, fc * 128:(fc + 1) * 128], idbf[:],
                          after=tmax(tyb, t_id, p.bank[bank], tok_xn0T_free),
                          done=True)
                te = p.op("vector", "tensor_copy",
                          xn0T[:, fc * MB + tt * 128: fc * MB + (tt + 1) * 128],
                          PSbf[bank][:, 0:128], after=[tm], done=True)
                p.bank[bank] = te
                tA.append(te)
        tok_A = tA[-1]
        if mb == 0 and "dbg_xn0T" in dbg:
            td_ = p.dma("sync", dbg["dbg_xn0T"][:], xn0T[:], after=[tok_A], sem="dbg")
            tok_xn0T_free = ("PAIR", tok_xn0T_free, td_)

        tok_attn_parts = []
        for hg in range(NHG):
            t_dd = p.dma("sync", ddhg[:], io["ddt"][:, hg * 2048:(hg + 1) * 2048],
                         after=[tok_ddhg], sem="ddh")
            for fc in range(16):
                t_wv_all = p.dma("gpsimd",
                                 wkv[:, fc * 512:(fc + 1) * 512]
                                 .rearrange("p (h r) -> p h r", h=HPG),
                                 io["qkvw_t"][fc * 128:(fc + 1) * 128, :]
                                 .rearrange("p (hh q) -> p hh q", q=192)
                                 [:, hg * HPG:(hg + 1) * HPG, 64:192],
                                 after=[tok_vt_war], sem="wkv")
            te_l = [None] * NT
            for tt in range(NT):
                bank = tt % 2
                for fc in range(16):
                    p.op("tensor", "matmul", PS[bank][:, 0:512],
                         xn0T[:, fc * MB + tt * 128: fc * MB + (tt + 1) * 128],
                         wkv[:, fc * 512:(fc + 1) * 512],
                         start=(fc == 0), stop=(fc == 15),
                         after=tmax(t_wv_all, tok_A, p.bank[bank]) if fc == 0
                         else None,
                         done=(fc == 15))
                tok_mm = ("eT", p.cnt["eT"])
                te = p.op("scalar", "activation", kvt[:, tt * 512:(tt + 1) * 512],
                          PS[bank][:, 0:512], AF.Silu,
                          after=tmax(tok_mm, tok_kvt_dve_war), done=True)
                p.bank[bank] = te
                te_l[tt] = te
                dr()
            tok_vt_ready = te_l[NT - 1]

            tq_h = [None] * HPG
            tok_proj_last = [None] * HPG

            def emit_projhead(hh):
                h = hg * HPG + hh
                par = hh % 2
                qfh = qf[:, hh * MB:(hh + 1) * MB]
                tw = p.dma("gpsimd",
                           whead[par][:].rearrange("p (c o) -> p c o", c=16),
                           io["qkvw_t"][:, 192 * h:192 * h + 128]
                           .rearrange("(c p) o -> p c o", p=128),
                           after=[tok_whead[par]], sem=f"wh{par}")
                bank = par
                for fc in range(16):
                    p.op("tensor", "matmul", PS[bank][:, 0:512],
                         whead[par][:, fc * 128:(fc + 1) * 128],
                         xn0T[:, fc * MB:fc * MB + MB],
                         start=(fc == 0), stop=(fc == 15),
                         after=tmax(tw, tok_A, p.bank[bank], tok_qk_war[hh])
                         if fc == 0 else None,
                         done=(fc == 15))
                tok_mm = ("eT", p.cnt["eT"])
                tok_whead[par] = tok_mm
                tq = p.op("scalar", "activation", qfh, PS[bank][0:64, 0:512],
                          AF.Silu, after=[tok_mm], done=True)
                p.bank[bank] = tq
                tq_h[hh] = tq

            def emit_light(hh):
                nonlocal tok_ktkd_war, tok_kvt_dve_war, tok_vt_war
                nonlocal tok_attn_free
                h = hg * HPG + hh
                tq = tq_h[hh]
                for blk in range(NBLK):
                    lpar = (h * NBLK + blk) % 2
                    c0b = hh * MB + blk * 256
                    t_kf = []
                    for jc in range(2):
                        tkt = p.op("tensor", "transpose", PSbf[6][0:64, 0:128],
                                   kvt[:, (blk * 2 + jc) * 512 + hh * 128:
                                       (blk * 2 + jc) * 512 + hh * 128 + 64],
                                   idbf[:],
                                   after=tmax(te_l[blk * 2 + jc], t_id,
                                              p.bank[6], tok_qk_war[hh]),
                                   done=True)
                        tke = p.op("vector", "tensor_copy",
                                   kf[:, c0b + jc * 128: c0b + (jc + 1) * 128],
                                   PSbf[6][0:64, 0:128], after=[tkt], done=True)
                        p.bank[6] = tke
                        t_kf.append(tke)
                    t_awb = []
                    for jc in range(2):
                        bank_aw = 2 + jc
                        tm = p.op("tensor", "matmul", PS[bank_aw][:, 0:256],
                                  kf[:, c0b + jc * 128: c0b + (jc + 1) * 128],
                                  qf[:, c0b: c0b + 256],
                                  start=True, stop=True,
                                  after=tmax(tq, t_kf[jc], t_dd, p.bank[bank_aw]),
                                  done=True)
                        tdd = p.op("vector", "tensor_tensor",
                                   awb[:, lpar * 512 + jc * 256:
                                       lpar * 512 + (jc + 1) * 256],
                                   PS[bank_aw][:, 0:256],
                                   ddhg[:, (hh * 2 + jc) * 256:(hh * 2 + jc + 1) * 256],
                                   ALU.mult, after=tmax(tm, tok_awb_dump[0]),
                                   done=True)
                        p.bank[bank_aw] = tdd
                        t_awb.append(tdd)
                    for ic in range(2):
                        bank_at = 4 + ic
                        tt_g = blk * 2 + ic
                        for jc in range(2):
                            p.op("tensor", "matmul", PS[bank_at][:, 0:64],
                                 awb[:, lpar * 512 + jc * 256 + ic * 128:
                                     lpar * 512 + jc * 256 + (ic + 1) * 128],
                                 kvt[:, (blk * 2 + jc) * 512 + hh * 128 + 64:
                                     (blk * 2 + jc) * 512 + hh * 128 + 128],
                                 start=(jc == 0), stop=(jc == 1),
                                 after=tmax(t_awb[jc], te_l[blk * 2 + jc],
                                            p.bank[bank_at]),
                                 done=(jc == 1))
                        t_intra = ("eT", p.cnt["eT"])
                        tmi = p.op("tensor", "matmul", PS[6][:, 0:64],
                                   qf[:, c0b + ic * 128: c0b + (ic + 1) * 128],
                                   stbf[:, h * 64:(h + 1) * 64],
                                   start=True, stop=True,
                                   after=tmax(tq, tok_st_h[h], p.bank[6]),
                                   done=True)
                        tei = p.op("scalar", "activation",
                                   tin[:, ic * 64:(ic + 1) * 64],
                                   PS[6][:, 0:64], AF.Copy,
                                   scale=qdkd[:, h * 2 + ic: h * 2 + ic + 1],
                                   after=tmax(tmi, t_qdkd, tok_tin_war[ic]),
                                   done=True)
                        p.bank[6] = tei
                        tat = p.op("vector", "tensor_tensor",
                                   attn[:, tt_g * 2048 + h * 64:
                                        tt_g * 2048 + (h + 1) * 64],
                                   PS[bank_at][:, 0:64],
                                   tin[:, ic * 64:(ic + 1) * 64],
                                   ALU.add, after=tmax(t_intra, tei, tok_attn_free),
                                   done=True)
                        tok_tin_war[ic] = tat
                        p.bank[bank_at] = tat
                        tok_attn_parts.append(tat)
                    t_ktkd = []
                    for jc in range(2):
                        tkd = p.op("vector", "tensor_scalar",
                                   ktkd[:, jc * 64:(jc + 1) * 64],
                                   kvt[:, (blk * 2 + jc) * 512 + hh * 128:
                                       (blk * 2 + jc) * 512 + hh * 128 + 64],
                                   qdkd[:, 64 + h * 2 + jc: 64 + h * 2 + jc + 1],
                                   None, ALU.mult,
                                   after=tmax(te_l[blk * 2 + jc], t_qdkd,
                                              tok_ktkd_war),
                                   done=True)
                        t_ktkd.append(tkd)
                    for jc in range(2):
                        p.op("tensor", "matmul", PS[7][0:64, 0:64],
                             ktkd[:, jc * 64:(jc + 1) * 64],
                             kvt[:, (blk * 2 + jc) * 512 + hh * 128 + 64:
                                 (blk * 2 + jc) * 512 + hh * 128 + 128],
                             start=(jc == 0), stop=(jc == 1),
                             after=tmax(t_ktkd[0], t_ktkd[1],
                                        te_l[blk * 2 + 1],
                                        p.bank[7]) if jc == 0 else None,
                             done=(jc == 1))
                    t_smm = ("eT", p.cnt["eT"])
                    tok_ktkd_war = t_smm
                    tsu = p.op("vector", "scalar_tensor_tensor",
                               state[:, h * 64:(h + 1) * 64],
                               state[:, h * 64:(h + 1) * 64], float(bd[h]),
                               PS[7][0:64, 0:64], ALU.mult, ALU.add,
                               after=tmax(t_smm, tok_st_h[h]), done=True)
                    tok_st_h[h] = p.op("vector", "tensor_copy",
                                       stbf[:, h * 64:(h + 1) * 64],
                                       state[:, h * 64:(h + 1) * 64],
                                       after=[tsu], done=True)
                    p.bank[7] = tok_st_h[h]
                    dr()
                tok_qk_war[hh] = t_smm
                tok_proj_last[hh] = t_awb

            emit_projhead(0)
            for hh in range(HPG):
                if hh + 1 < HPG:
                    emit_projhead(hh + 1)
                emit_light(hh)
            t_awb = tok_proj_last[HPG - 1]
            if mb == 0 and hg == 0 and "dbg_qf" in dbg:
                td1_ = p.dma("sync", dbg["dbg_qf"][:], qf[:], after=[tok_st_h[H - 1]], sem="dbg")
                td2_ = p.dma("sync", dbg["dbg_kf"][:], kf[:], after=[tok_st_h[H - 1]], sem="dbg")
                td3_ = p.dma("sync", dbg["dbg_kvt"][:], kvt[:], after=[tok_st_h[H - 1]], sem="dbg")
                for _i in range(HPG):
                    tok_qk_war[_i] = ("PAIR", tok_qk_war[_i], td1_, td2_)
                tok_vt_war = ("PAIR", tok_vt_war, td3_)
                tok_kvt_dve_war = ("PAIR", tok_kvt_dve_war, td3_)
            if mb == 0 and hg == 1 and "dbg_kfl" in dbg:
                tdl_ = p.dma("sync", dbg["dbg_kfl"][:], kf[:], after=[tok_st_h[H - 1]], sem="dbg")
                for _i in range(HPG):
                    tok_qk_war[_i] = ("PAIR", tok_qk_war[_i], tdl_)
            tok_vt_war = tok_qk_war[HPG - 1]
            tok_kvt_dve_war = tok_qk_war[HPG - 1]
            tok_ddhg = t_awb[-1]
        tok_attn_done = tok_attn_parts[-1]
        if mb == 0 and "dbg_attn" in dbg:
            tda_ = p.dma("sync", dbg["dbg_attn"][:], attn[:], after=[tok_attn_done], sem="dbg")
            tok_attn_dump = tda_
        else:
            tok_attn_dump = None

        trec = []
        for tt in range(NT):
            tsq = p.op("scalar", "activation", p.junkb[:],
                       attn[:, tt * 2048:(tt + 1) * 2048], AF.Square,
                       accum_out=lncol[:, tt:tt + 1],
                       after=[tok_attn_done], done=True)
            tsd = p.op("scalar", "activation", lncol[:, 8 + tt: 9 + tt],
                       lncol[:, tt:tt + 1], AF.Sqrt, bias=ln_eps, scale=inv2048,
                       after=[tsq], done=True)
            tr = p.op("vector", "reciprocal", rstdc[:, tt:tt + 1],
                      lncol[:, 8 + tt:9 + tt], after=[tsd], done=True)
            trec.append(tr)

        tok_wg = tok_wstkA
        tga = None
        tn = None
        tga_l = [[None] * NT for _ in range(NOC)]
        for oc in range(NOC):
            twl = p.dma("gpsimd", wstkA[:].rearrange("p (c o) -> p c o", c=16),
                        io["gatew_t"][:, oc * 512:(oc + 1) * 512]
                        .rearrange("(c p) o -> p c o", p=128), after=[tok_wg],
                        sem="wsa")
            for tt in range(NT):
                bank = tt % 2
                for fc in range(16):
                    p.op("tensor", "matmul", PS[bank][:, 0:512],
                         xn0T[:, fc * MB + tt * 128: fc * MB + (tt + 1) * 128],
                         wstkA[:, fc * 512:(fc + 1) * 512],
                         start=(fc == 0), stop=(fc == 15),
                         after=tmax(twl, tok_A, p.bank[bank]) if fc == 0 else None,
                         done=(fc == 15))
                tok_mm = ("eT", p.cnt["eT"])
                tg = p.op("scalar", "activation", gateb[:], PS[bank][:, 0:512],
                          AF.Sigmoid, after=tmax(tok_mm, tn), done=True)
                p.bank[bank] = tg
                tn = p.op("vector", "tensor_tensor", nwg[:], gateb[:],
                          nwb[:, oc * 512:(oc + 1) * 512], ALU.mult,
                          after=tmax(tg, t_nwb), done=True)
                tga = p.op("vector", "scalar_tensor_tensor",
                           ga[:, tt * 2048 + oc * 512: tt * 2048 + (oc + 1) * 512],
                           attn[:, tt * 2048 + oc * 512: tt * 2048 + (oc + 1) * 512],
                           rstdc[:, tt:tt + 1], nwg[:], ALU.mult, ALU.mult,
                           after=tmax(tn, trec[tt], tok_ga_free), done=True)
                tga_l[oc][tt] = tga
                dr()
            tok_wg = tok_mm
        tok_gate_done = tok_mm

        t_gaT = []
        gaT_tt = [None] * NT
        for tt in range(NT):
            for ac in range(16):
                bank = 2 + (ac % 2)
                tm = p.op("tensor", "transpose", PSbf[bank][:, 0:128],
                          ga[:, tt * 2048 + ac * 128: tt * 2048 + (ac + 1) * 128],
                          idbf[:],
                          after=tmax(tga_l[ac // 4][tt], p.bank[bank],
                                     tok_attn_dump),
                          done=True)
                te = p.op("vector", "tensor_copy",
                          gaT[:, ac * MB + tt * 128: ac * MB + (tt + 1) * 128],
                          PSbf[bank][:, 0:128], after=[tm], done=True)
                p.bank[bank] = te
                t_gaT.append(te)
            gaT_tt[tt] = te
        tok_gaT = t_gaT[-1]
        tok_ga_free = tok_gaT
        if mb == 0 and "dbg_ga" in dbg:
            tdg_ = p.dma("sync", dbg["dbg_ga"][:], ga[:], after=[tok_gaT], sem="dbg")
            tok_ga_free = ("PAIR", tok_ga_free, tdg_)

        t_xdq = []
        for tt in range(NT):
            par = tt % 2
            row0 = tokbase + tt * 128
            tl = p.dma("sync", x0t[par][:], io["x0"][row0:row0 + 128, :],
                       after=[tok_lnbuf[par]])
            _, tsq2, txn = emit_ln(p, None, xnf[:], x0t[par][:], lncol,
                                   par * 8, ln_eps, [tl], want_f32=True)
            tok_lnbuf[par] = ("PAIR", txn, tsq2)
            am = amcol[:, par * 64: par * 64 + 16]
            rcp_ = amcol[:, par * 64 + 16: par * 64 + 32]
            sc224 = amcol[:, par * 64 + 32: par * 64 + 48]
            tred = p.op("vector", "tensor_reduce", am,
                        xnf[:].rearrange("p (b i) -> p b i", i=128),
                        AX.X, ALU.max, True, after=[txn, tok_cbuf[par]],
                        done=True)
            trcp = p.op("vector", "reciprocal", rcp_, am, after=[tred], done=True)
            tsc = p.op("vector", "tensor_single_scalar", sc224, rcp_, 224.0,
                       ALU.mult, after=[trcp], done=True)
            t8x = None
            for b in range(16):
                t8x = p.op("vector", "tensor_scalar",
                           q8[par][:, b * 128:(b + 1) * 128],
                           xnf[:, b * 128:(b + 1) * 128], sc224[:, b:b + 1],
                           None, ALU.mult,
                           after=[tok_cbuf[par], tsc] if b == 0 else None,
                           done=(b == 15))
            tds = p.op("vector", "tensor_single_scalar", sc224, am, 1.0 / 224.0,
                       ALU.mult, after=[tred], done=True)
            tdq_l = []
            for b in range(16):
                tdqx = p.op("scalar", "activation",
                            xnb[par][:, b * 128:(b + 1) * 128],
                            q8[par][:, b * 128:(b + 1) * 128], AF.Copy,
                            scale=sc224[:, b:b + 1],
                            after=tmax(t8x, tds, tok_A, tok_lnbuf[par])
                            if b == 0 else None,
                            done=True)
                tdq_l.append(tdqx)
            tok_cbuf[par] = tdqx
            dr()
            for fc in range(16):
                bank = 2 + (fc % 2)
                tm = p.op("tensor", "transpose", PSbf[bank][:, 0:128],
                          xnb[par][:, fc * 128:(fc + 1) * 128], idbf[:],
                          after=tmax(tdq_l[fc], tok_gate_done, p.bank[bank]),
                          done=True)
                te = p.op("vector", "tensor_copy",
                          xdqT[:, fc * MB + tt * 128: fc * MB + (tt + 1) * 128],
                          PSbf[bank][:, 0:128], after=[tm], done=True)
                p.bank[bank] = te
                t_xdq.append(te)
        tok_xdq = t_xdq[-1]

        tok_wo = tok_gate_done
        tok_wf = None
        tok_x1buf = [None, None]
        tsg = None
        for oc in range(NOC):
            two = p.dma("gpsimd", wstkA[:].rearrange("p (c o) -> p c o", c=16),
                        io["outw_t"][:, oc * 512:(oc + 1) * 512]
                        .rearrange("(c p) o -> p c o", p=128), after=[tok_wo],
                        sem="wsa")
            for fc in range(16):
                t_wfl_all = p.dma_tp(wstkF[:, fc * 512:(fc + 1) * 512],
                                     io["wdeq_scr"][oc * 512:(oc + 1) * 512,
                                                    fc * 128:(fc + 1) * 128],
                                     after=[t_wdq_done, tok_wf], sem="wsf")
            for tt in range(NT):
                parx = tt % 2
                row0 = tokbase + tt * 128
                bs = tt % 2
                bg = 4 + tt % 2
                for ac in range(16):
                    p.op("tensor", "matmul", PS[bs][:, 0:512],
                         gaT[:, ac * MB + tt * 128: ac * MB + (tt + 1) * 128],
                         wstkA[:, ac * 512:(ac + 1) * 512],
                         start=(ac == 0), stop=(ac == 15),
                         after=tmax(two, gaT_tt[tt], p.bank[bs])
                         if ac == 0 else None,
                         done=(ac == 15))
                t_s0 = ("eT", p.cnt["eT"])
                for fc in range(16):
                    p.op("tensor", "matmul", PS[bg][:, 0:512],
                         xdqT[:, fc * MB + tt * 128: fc * MB + (tt + 1) * 128],
                         wstkF[:, fc * 512:(fc + 1) * 512],
                         start=(fc == 0), stop=(fc == 15),
                         after=tmax(t_wfl_all, tok_xdq, p.bank[bg])
                         if fc == 0 else None,
                         done=(fc == 15))
                t_g0 = ("eT", p.cnt["eT"])
                tse = p.op("scalar", "activation", s0e[:], PS[bs][:, 0:512],
                           AF.Copy, after=tmax(t_s0, tsg), done=True)
                p.bank[bs] = tse
                tge = p.op("scalar", "activation", g0e[:], PS[bg][:, 0:512],
                           AF.Silu, after=tmax(t_g0, tsg), done=True)
                p.bank[bg] = tge
                txl = p.dma("sync", x0sl[parx][:],
                            io["x0"][row0:row0 + 128, oc * 512:(oc + 1) * 512],
                            after=[tok_x1buf[parx]], sem=f"xs{parx}")
                tsg = p.op("vector", "tensor_tensor", sge[:], s0e[:], g0e[:],
                           ALU.mult, after=tmax(tse, tge), done=True)
                tx1 = p.op("vector", "tensor_tensor", x1b[parx][:], sge[:],
                           x0sl[parx][:], ALU.add, after=tmax(tsg, txl), done=True)
                ts = p.dma("sync", io["x1"][row0:row0 + 128, oc * 512:(oc + 1) * 512],
                           x1b[parx][:], after=[tx1], sem=f"x1s{parx}")
                tok_x1buf[parx] = ts
                last_x1_store = ts
                dr()
            tok_wo = t_s0
            tok_wf = t_g0
        tok_wstkA = t_s0
        tok_xn0T_free = t_g0
        tok_attn_free = t_s0

    dr(100)
    p.wait("sync", [tok_x1buf[0], tok_x1buf[1]])
    return last_x1_store





def gamma_gen_l0(p, io, bufs):
    wsg = bufs["wsg"]; gcl = bufs["gcl"]; gcc = bufs["gcc"]; gcr = bufs["gcr"]
    onesf1 = bufs["onesf1"]; ones128f = bufs["ones128f"]
    t_ms = bufs["t_ms"]; PSG = bufs["PSG"]; GB = bufs["gbank"]
    HW = 1024
    tok_sl = [None, None]
    gammas = []
    tok_red = None
    tok_gb = []
    for mi, (wkey, ntile, width) in enumerate(
            [("gwt", 16, 8192), ("vwt", 16, 8192), ("owt", 64, 2048)]):
        nchunk = width // HW
        cnt = 0
        npart = 0
        for i in range(ntile):
            for ch in range(nchunk):
                par = cnt % 2
                tile = wsg[:, par * HW:(par + 1) * HW]
                tl = p.dma("sync", tile,
                           io[wkey][i * 128:(i + 1) * 128,
                                    ch * HW:(ch + 1) * HW],
                           after=[tok_sl[par]], sem=f"sg{par}")
                tok_sl[par] = p.op("vector", "tensor_reduce",
                                   gcl[:, cnt % 64:cnt % 64 + 1],
                                   tile, AX.X, ALU.add, True,
                                   after=tmax(tl, tok_red if cnt % 64 == 0
                                              else None), done=True)
                cnt += 1
                if cnt % 64 == 0:
                    tok_red = p.op("vector", "tensor_reduce",
                                   gcl[:, 64 + npart:65 + npart],
                                   gcl[:, 0:64], AX.X, ALU.add,
                                   after=[tok_sl[0], tok_sl[1]], done=True)
                    npart += 1
                yield
        tmt = p.op("vector", "tensor_reduce", gcl[:, 70 + mi:71 + mi],
                   gcl[:, 64:64 + npart], AX.X, ALU.add,
                   after=[tok_red], done=True)
        tpsum = p.op("tensor", "matmul", PSG[0:1, 0:1],
                     gcl[:, 70 + mi:71 + mi], ones128f[:], start=True,
                     stop=True, after=tmax(tmt, t_ms, p.bank[GB]), done=True)
        t1_ = p.op("vector", "tensor_single_scalar",
                   gcr[0:1, 8 + mi * 3:9 + mi * 3],
                   PSG[0:1, 0:1], 1.0 / 16777216.0, ALU.mult, after=[tpsum],
                   done=True)
        p.bank[GB] = t1_
        t2_ = p.op("vector", "tensor_single_scalar",
                   gcr[0:1, 9 + mi * 3:10 + mi * 3],
                   gcr[0:1, 8 + mi * 3:9 + mi * 3], 1e-5, ALU.max,
                   after=[t1_], done=True)
        t3_ = p.op("vector", "reciprocal", gcr[0:1, 10 + mi * 3:11 + mi * 3],
                   gcr[0:1, 9 + mi * 3:10 + mi * 3], after=[t2_], done=True)
        gammas.append((9 + mi * 3, 10 + mi * 3, t2_, t3_))
        yield
    for mi in range(3):
        gcolx, grcolx, tg_, tr_ = gammas[mi]
        tmmg = p.op("tensor", "matmul", PSG[:, 0:2],
                    onesf1[:], gcr[0:1, gcolx:gcolx + 2], start=True,
                    stop=True, after=tmax(tg_, tr_, t_ms, p.bank[GB]),
                    done=True)
        tevg = p.op("vector", "tensor_copy", gcc[:, 8 + mi * 2:10 + mi * 2],
                    PSG[:, 0:2], after=[tmmg], done=True)
        p.bank[GB] = tevg
        tok_gb.append(tevg)
        yield
    tok_gsc = p.dma("sync", io["gsc"][:, 0:16], gcc[:, 0:16],
                    after=tok_gb, sem="gsc")

    brs = wsg[0:1, 0:HW]
    bjunk = wsg.bitcast(BF16)[0:1, 2 * HW:3 * HW]
    tok_bs = ("PAIR", tok_sl[0], tok_sl[1])
    for mi, bkey, brow in [(0, "gb", 0), (1, "vb", 1)]:
        for ch in range(8):
            tbl = p.dma("sync", brs, io[bkey][0:1, ch * HW:(ch + 1) * HW],
                        after=[tok_bs], sem="bias")
            tsc_ = p.op("vector", "tensor_scalar", bjunk,
                        brs, gcr[0:1, gammas[mi][1]:gammas[mi][1] + 1],
                        None, ALU.mult, after=tmax(tbl, gammas[mi][3]),
                        done=True)
            tok_bs = p.dma("sync",
                           io["wbias"][brow:brow + 1, ch * HW:(ch + 1) * HW],
                           bjunk, after=[tsc_], sem="bias")
            yield
    for ch in range(2):
        tbl3 = p.dma("sync", brs, io["ob"][0:1, ch * HW:(ch + 1) * HW],
                     after=[tok_bs], sem="bias")
        tbo_ = p.op("vector", "tensor_scalar", bjunk, brs,
                    gcr[0:1, gammas[2][1]:gammas[2][1] + 1], None, ALU.mult,
                    after=tmax(tbl3, gammas[2][3]), done=True)
        tok_bs = p.dma("sync", io["wbias"][2:3, ch * HW:(ch + 1) * HW],
                       bjunk, after=[tbo_], sem="bias")
        yield
    bufs["tok_done"] = ("PAIR", tok_gsc, tok_bs)
    while True:
        yield

def gamma_quant_gen(p, io, bufs):
    wga = bufs["wga"]; q8g = bufs["q8g"]; j8g = bufs["j8g"]
    gccol = bufs["gccol"]

    tok_gs = p.dma("sync", gccol[:, 0:16], io["gsc"][:, 0:16], sem="gsc")
    GRg = lambda mi: gccol[:, 9 + mi * 2: 10 + mi * 2]
    tok_gt = [None, None]
    tok_wq8 = [None, None]
    for mi, (wkey, skey, ntile, width) in enumerate(
            [("gwt", "wgi8", 16, 8192), ("vwt", "wvi8", 16, 8192),
             ("owt", "woi8", 64, 2048)]):
        nhalf = 2 if width == 8192 else 1
        hw = width // nhalf
        cnt = 0
        for i in range(ntile):
            for half in range(nhalf):
                par = cnt % 2
                cnt += 1
                tile = wga[:, par * 4096: par * 4096 + hw]
                tl = p.dma("sync", tile,
                           io[wkey][i * 128:(i + 1) * 128,
                                    half * hw:(half + 1) * hw],
                           after=[tok_gt[par], tok_wq8[par]], sem=f"gt{par}")
                tq1 = p.op("vector", "tensor_scalar", q8g[:, 0:hw], tile,
                           GRg(mi), None, ALU.mult,
                           after=tmax(tl, tok_gs, tok_wq8[par]), done=True)
                tq2 = p.op("vector", "tensor_scalar", j8g[:, 0:hw],
                           q8g[:, 0:hw], 1.0, -1.0, ALU.min, ALU.max,
                           after=[tq1], done=True)
                tok_gt[par] = tq2
                tok_wq8[par] = p.dma(
                    "sync",
                    io[skey][i * 128:(i + 1) * 128, half * hw:(half + 1) * hw],
                    j8g[:, 0:hw], after=[tq2], sem=f"w8s{par}")
                yield
    bufs["tok_w8_done"] = ("PAIR", tok_wq8[0], tok_wq8[1])
    while True:
        yield


def build_l1a(p, io, dbg=None, side=True):
    dbg = dbg or {}
    NH = 16

    xn1T = p.sb("xn1T", [128, 16 * 2048], BF16)
    qh = [p.sb(f"qh{i}", [128, 2048], BF16) for i in range(2)]
    mall2 = p.sb("mall", [1, 4096], BF16)
    kro = p.sb("kro", [128, 2048], BF16)
    vsb = p.sb("vsb", [128, 16 * 136], BF16)
    ksb = p.sb("ksb", [128, 2048], BF16)
    tband = p.sb("s_tband", [128, 1024], BF16)
    rcq = p.sb("s_rcq", [128, 2048], BF16)
    rsq = p.sb("s_rsq", [128, 2048], BF16)
    rck = p.sb("s_rck", [128, 2048], BF16)
    rsk = p.sb("s_rsk", [128, 2048], BF16)
    pswap = p.sb("s_pswap", [128, 128], BF16)
    idbf = p.sb("s_idbf", [128, 128], BF16)
    idf32 = p.sb("s_idf32", [128, 128], F32)
    wkb = p.sb("wkb", [128, 2048], BF16)
    wvb = p.sb("wvb", [128, 2048], BF16)
    wqb = [p.sb(f"wqb{i}", [128, 2048], BF16) for i in range(2)]
    tmp1 = p.sb("tmp1", [128, 512], F32)
    tmp2 = p.sb("tmp2", [128, 512], F32)
    sqt = p.sb("sqt", [128, 512], BF16)
    krow = p.sb("krow", [1, 2048], BF16)
    nq2 = p.sb("nq2", [1, 4096], BF16)
    crow = p.sb("crow", [1, 8], F32)
    onesb = p.sb("onesb1", [1, 128], BF16)
    negob = p.sb("negob", [1, 128], BF16)
    ones128b = p.sb("ones128b", [128, 1], BF16)
    cpool = p.sb("cpool1", [128, 8], F32)
    lncol = p.sb("lncol1", [128, 16], F32)
    p.junkb = p.sb("junkb1", [128, 2048], BF16)
    psb_ = [p.sb(f"psb{i}", [128, 512], BF16) for i in range(4)]
    pmk2 = p.sb("pmk", [128, 1024], BF16)
    s1t = [p.sb(f"s1t{i}", [128, 512], BF16) for i in range(2)]
    rcol = p.sb("rcol", [128, 8], F32)
    x1t = [p.sb(f"x1t{i}", [128, 2048], BF16) for i in range(2)]
    xnb1 = [p.sb(f"xnb1_{i}", [128, 2048], BF16) for i in range(2)]
    wga = p.sb("wga", [128, 2 * 4096], F32)
    q8g = x1t[0].bitcast(I8)
    j8g = x1t[1].bitcast(I8)
    gccol = p.sb("gccol", [128, 16], F32)

    PS = [p.ps(f"PT{b}", [128, 512], F32) for b in range(8)]
    PSbf = [ps.bitcast(BF16) for ps in PS]
    ln_eps = cpool[:, 0:1]

    p.dma("sync", tband[:], io["tband"][:], sem="init")
    p.dma("sync", rcq[:], io["rcq"][:], sem="init")
    p.dma("sync", rsq[:], io["rsq"][:], sem="init")
    p.dma("sync", rck[:], io["rck"][:], sem="init")
    p.dma("sync", rsk[:], io["rsk"][:], sem="init")
    p.dma("sync", pswap[:], io["pswap"][:], sem="init")
    p.dma("sync", idbf[:], io["idbf"][:], sem="init")
    p.dma("sync", idf32[:], io["idf32"][:], sem="init")
    p.dma("gpsimd", wkb[:].rearrange("p (c o) -> p c o", c=16),
          io["wk"][:].rearrange("(c p) o -> p c o", p=128), sem="init")
    tok_init = p.dma("gpsimd", wvb[:].rearrange("p (c o) -> p c o", c=16),
                     io["wv"][:].rearrange("(c p) o -> p c o", p=128), sem="init")
    p.op("vector", "memset", cpool[:, 0:1], 1e-5)
    p.op("vector", "memset", onesb[:], 1.0)
    p.op("vector", "memset", negob[:], -1.0)
    t_msv = p.op("vector", "memset",
                 vsb[:].rearrange("p (t c) -> p t c", c=136)[:, :, 128:129],
                 1.0, done=True)
    t_ms = p.op("vector", "memset", ones128b[:], 1.0, done=True)
    gbufs = dict(wga=wga, q8g=q8g, j8g=j8g, gccol=gccol)
    gq = gamma_quant_gen(p, io, gbufs) if side else None

    tok_lnb = [None, None]
    tA = None
    for tt in range(16):
        par = tt % 2
        tl = p.dma("sync", x1t[par][:], io["x1"][tt * 128:(tt + 1) * 128, :],
                   after=[tok_lnb[par]], sem=f"x1l{par}")
        tyb, tsq, _ = emit_ln(p, xnb1[par][:], None, x1t[par][:], lncol,
                              par * 8, ln_eps, [tl])
        tok_lnb[par] = ("PAIR", tyb, tsq)
        if gq is not None:
            next(gq)
            next(gq)
        for fc in range(16):
            bank = 2 + (fc % 2)
            tm = p.op("tensor", "transpose", PSbf[bank][:, 0:128],
                      xnb1[par][:, fc * 128:(fc + 1) * 128], idbf[:],
                      after=tmax(tyb, tok_init, p.bank[bank]), done=True)
            tA = p.op("vector", "tensor_copy",
                      xn1T[:, fc * 2048 + tt * 128: fc * 2048 + (tt + 1) * 128],
                      PSbf[bank][:, 0:128], after=[tm], done=True)
            p.bank[bank] = tA
    tok_xn1T = tA

    def rope_combine(dst, raw_ap, swap_ps, ct, st, c512, after_toks):
        ta = p.op("vector", "tensor_tensor", tmp1[:], raw_ap,
                  ct[:, c512 * 512:(c512 + 1) * 512], ALU.mult,
                  after=after_toks, done=True)
        tb = p.op("vector", "tensor_tensor", tmp2[:], swap_ps,
                  ct[:, 0:0] if False else st[:, c512 * 512:(c512 + 1) * 512],
                  ALU.mult, after=after_toks, done=True)
        return p.op("vector", "tensor_tensor", dst, tmp1[:], tmp2[:],
                    ALU.add, after=tmax(ta, tb), done=True)

    tok_kevac = [None] * 4
    tok_krope = [None] * 4
    tok_sqt = None
    tok_nk = None
    for kc in range(4):
        bank = kc % 2
        for fc in range(16):
            p.op("tensor", "matmul", PS[bank][:, 0:512],
                 wkb[:, fc * 128:(fc + 1) * 128],
                 xn1T[:, fc * 2048 + kc * 512: fc * 2048 + (kc + 1) * 512],
                 start=(fc == 0), stop=(fc == 15),
                 after=tmax(tok_init, tok_xn1T, p.bank[bank]) if fc == 0
                 else None, done=(fc == 15))
        tmm = ("eT", p.cnt["eT"])
        te = p.op("scalar", "activation", ksb[:, kc * 512:(kc + 1) * 512],
                  PS[bank][:, 0:512], AF.Copy, after=[tmm], done=True)
        p.bank[bank] = te
        tok_kevac[kc] = te
        if gq is not None:
            next(gq)
    for kc in range(4):
        bank = kc % 2
        tsw = p.op("tensor", "matmul", PS[bank][:, 0:512], pswap[:],
                   ksb[:, kc * 512:(kc + 1) * 512], start=True, stop=True,
                   after=tmax(tok_kevac[kc], tok_init, p.bank[bank]), done=True)
        trp = rope_combine(kro[:, kc * 512:(kc + 1) * 512],
                           ksb[:, kc * 512:(kc + 1) * 512],
                           PS[bank][:, 0:512], rck, rsk, kc,
                           tmax(tsw, tok_kevac[kc], tok_init))
        p.bank[bank] = trp
        tok_krope[kc] = trp
        tsq_ = p.op("scalar", "activation", sqt[:],
                    kro[:, kc * 512:(kc + 1) * 512], AF.Square,
                    after=tmax(trp, tok_sqt), done=True)
        tnm = p.op("tensor", "matmul", PS[2][0:1, 0:512], ones128b[:],
                   sqt[:], start=True, stop=True,
                   after=tmax(tsq_, t_ms, p.bank[2]), done=True)
        tok_sqt = tnm
        tok_nk = p.op("vector", "tensor_copy",
                      krow[0:1, kc * 512:(kc + 1) * 512],
                      PS[2][0:1, 0:512], after=[tnm], done=True)
        p.bank[2] = tok_nk
    tgm = p.op("vector", "tensor_reduce", crow[0:1, 0:1], krow[0:1, 0:2048],
               AX.X, ALU.max, after=[tok_nk], done=True)
    tG = p.op("scalar", "activation", crow[0:1, 1:2], crow[0:1, 0:1], AF.Sqrt,
              after=[tgm], done=True)

    for tt in range(16):
        bank = tt % 2
        for fc in range(16):
            p.op("tensor", "matmul", PS[bank][:, 0:128],
                 xn1T[:, fc * 2048 + tt * 128: fc * 2048 + (tt + 1) * 128],
                 wvb[:, fc * 128:(fc + 1) * 128],
                 start=(fc == 0), stop=(fc == 15),
                 after=tmax(tok_init, tok_xn1T, p.bank[bank]) if fc == 0
                 else None, done=(fc == 15))
        tmm = ("eT", p.cnt["eT"])
        tok_v = p.op("scalar", "activation",
                     vsb[:, tt * 136:tt * 136 + 128],
                     PS[bank][:, 0:128], AF.Copy, after=[tmm], done=True)
        p.bank[bank] = tok_v

    tok_wq = [None, None]
    tok_ksbq = list(tok_krope)
    tok_nq_reader = [None, None]
    tok_m_reader = [None, None]
    tok_m_h = [None, None]
    tok_qrope_h = [None, None]
    tok_qh = [None, None]
    tok_psb = [None, None, None, None]
    tok_pmk = [None, None]
    tok_rcol_war = None
    tok_s1_store = [None, None]
    tw_h = [None, None]
    tok_sqt_ref = [tok_sqt]

    def emit_proj_qc(h, qc):
        par = h % 2
        b = qc % 2
        for fc in range(16):
            p.op("tensor", "matmul", PS[b][:, 0:512],
                 wqb[par][:, fc * 128:(fc + 1) * 128],
                 xn1T[:, fc * 2048 + qc * 512: fc * 2048 + (qc + 1) * 512],
                 start=(fc == 0), stop=(fc == 15),
                 after=tmax(tw_h[par], tok_xn1T, p.bank[b]) if fc == 0
                 else None, done=(fc == 15))
        tmm = ("eT", p.cnt["eT"])
        if qc == 3:
            tok_wq[par] = tmm
        te = p.op("vector", "tensor_copy", ksb[:, qc * 512:(qc + 1) * 512],
                  PS[b][:, 0:512],
                  after=tmax(tmm, tok_ksbq[qc]), done=True)
        p.bank[b] = te
        tsw = p.op("tensor", "matmul", PS[b][:, 0:512], pswap[:],
                   ksb[:, qc * 512:(qc + 1) * 512], start=True, stop=True,
                   after=tmax(te, tok_init, p.bank[b]), done=True)
        trp = rope_combine(
            qh[par][:, qc * 512:(qc + 1) * 512],
            ksb[:, qc * 512:(qc + 1) * 512],
            PS[b][:, 0:512], rcq, rsq, qc,
            tmax(tsw, te, tok_qh[par]))
        tok_ksbq[qc] = trp
        tsq_ = p.op("vector", "tensor_tensor", sqt[:],
                    qh[par][:, qc * 512:(qc + 1) * 512],
                    qh[par][:, qc * 512:(qc + 1) * 512],
                    ALU.mult, after=tmax(trp, tok_sqt_ref[0]), done=True)
        tnm = p.op("tensor", "matmul", PS[b][0:1, 0:512], ones128b[:],
                   sqt[:], start=True, stop=True,
                   after=tmax(tsq_, t_ms, trp), done=True)
        tok_sqt_ref[0] = tnm
        tq_ = p.op("vector", "tensor_copy",
                   nq2[0:1, par * 2048 + qc * 512:
                       par * 2048 + (qc + 1) * 512],
                   PS[b][0:1, 0:512],
                   after=tmax(tnm, tok_nq_reader[par]), done=True)
        p.bank[b] = tq_
        return trp, tq_

    def emit_attn_qc(h, qc):
        nonlocal tok_rcol_war
        par = h % 2
        tok_qrope = tok_qrope_h[par]
        tok_m = tok_m_h[par]
        nkt = 4 * qc + 4
        tsm_l = [None] * nkt
        tok_last_mm = None

        def emit_S(kt):
            bank = 2 + (kt % 2)
            p.op("tensor", "matmul", PS[bank][:, 0:512],
                 kro[:, kt * 128:(kt + 1) * 128],
                 qh[par][:, qc * 512:(qc + 1) * 512],
                 start=True, stop=False,
                 after=tmax(tok_qrope, tok_m, p.bank[bank]))
            p.op("tensor", "matmul", PS[bank][:, 0:512], negob[:],
                 mall2[0:1, par * 2048 + qc * 512:
                       par * 2048 + (qc + 1) * 512],
                 start=False, stop=True, done=True)
            tsm_l[kt] = ("eT", p.cnt["eT"])

        emit_S(0)
        for kt in range(nkt):
            bank = 2 + (kt % 2)
            s = kt % 4
            texp = p.op("scalar", "activation", psb_[s][:],
                        PS[bank][:, 0:512], AF.Exp,
                        after=tmax(tsm_l[kt], tok_psb[s]), done=True)
            p.bank[bank] = texp
            src_p = psb_[s]
            tok_p = texp
            o = kt - 4 * qc
            if o >= 0:
                m2 = kt % 2
                tok_p = p.op("vector", "tensor_tensor",
                             pmk2[:, m2 * 512:(m2 + 1) * 512],
                             psb_[s][:],
                             tband[:, 384 - 128 * o: 896 - 128 * o],
                             ALU.mult,
                             after=tmax(texp, tok_init, tok_pmk[m2]),
                             done=True)
                src_p = pmk2[:, m2 * 512:(m2 + 1) * 512]
            if kt + 1 < nkt:
                emit_S(kt + 1)
            for qt in range(4):
                p.op("tensor", "matmul", PS[4 + qt][:, 0:129],
                     src_p[:, qt * 128:(qt + 1) * 128] if o >= 0
                     else src_p[:, qt * 128:(qt + 1) * 128],
                     vsb[:, kt * 136:kt * 136 + 129],
                     start=(kt == 0), stop=(kt == nkt - 1),
                     after=tmax(tok_p, t_msv,
                                p.bank[4 + qt] if kt == 0 else None),
                     done=(qt == 3))
            tok_last_mm = ("eT", p.cnt["eT"])
            if o >= 0:
                tok_pmk[kt % 2] = tok_last_mm
            tok_psb[s] = tok_last_mm
        par2 = (h * 4 + qc) % 2
        tsc_l = []
        for qt in range(4):
            trc = p.op("vector", "reciprocal", rcol[:, qt:qt + 1],
                       PS[4 + qt][:, 128:129],
                       after=tmax(tok_last_mm, tok_rcol_war), done=True)
            tsc = p.op("vector", "tensor_scalar",
                       s1t[par2][:, qt * 128:(qt + 1) * 128],
                       PS[4 + qt][:, 0:128], rcol[:, qt:qt + 1], None,
                       ALU.mult,
                       after=tmax(trc, tok_s1_store[par2]),
                       done=True)
            p.bank[4 + qt] = tsc
            tsc_l.append(tsc)
        tok_rcol_war = tsc_l[-1]
        tok_s1_store[par2] = p.dma(
            "sync",
            io["s1"][qc * 512:(qc + 1) * 512, h * 128:(h + 1) * 128]
            .rearrange("(qt pp) d -> pp qt d", pp=128),
            s1t[par2][:].rearrange("pp (qt d) -> pp qt d", qt=4),
            after=tsc_l, sem=f"s1s{par2}")
        return tok_last_mm

    for hi in range(NH + 1):
        par = hi % 2
        if hi < NH:
            tw_h[par] = p.dma(
                "gpsimd", wqb[par][:].rearrange("p (c o) -> p c o", c=16),
                io["wq"][:, hi * 128:(hi + 1) * 128]
                .rearrange("(c p) o -> p c o", p=128),
                after=[tok_wq[par]], sem=f"wq{par}")
        trp_last = None
        tq_l = []
        for qc in range(4):
            if hi < NH:
                trp_last, tq_ = emit_proj_qc(hi, qc)
                tq_l.append(tq_)
            if hi > 0:
                tok_last = emit_attn_qc(hi - 1, qc)
            if gq is not None:
                for _ in range(2):
                    next(gq)
        if hi > 0:
            apar = (hi - 1) % 2
            tok_qh[apar] = tok_last
            tok_m_reader[apar] = tok_last
        if hi < NH:
            tqn = p.op("scalar", "activation",
                       mall2[0:1, par * 2048:(par + 1) * 2048],
                       nq2[0:1, par * 2048:(par + 1) * 2048], AF.Sqrt,
                       scale=crow[0:1, 0:1],
                       after=tmax(*tq_l, tgm, tok_m_reader[par]), done=True)
            tok_nq_reader[par] = tqn
            tok_m_h[par] = tqn
            tok_qrope_h[par] = trp_last
        if gq is not None:
            next(gq)
    if gq is not None:
        for _ in range(20):
            next(gq)
    p.wait("sync", [tok_s1_store[0], tok_s1_store[1]])
    return tok_s1_store


def build_l1b(p, io, dbg=None):
    dbg = dbg or {}
    TG = 512
    NG = 2048 // TG
    NTT = TG // 128

    wga = p.sb("wga", [128, 2 * 4096], F32)
    wtb = [wga.bitcast(BF16)[:, i * 8192:(i + 1) * 8192] for i in range(2)]
    gvsb = p.sb("gvsb", [128, 4 * 8192], BF16)
    xqT = p.sb("xqT", [128, 16 * 512], BF16)
    gsb = p.sb("gsb", [128, 4 * 512], BF16)
    vsb2 = p.sb("vsb2", [128, 4 * 512], BF16)
    xqrow = p.sb("xqrow", [1, 512], BF16)
    r3row = p.sb("r3row", [1, 512], BF16)
    borow = p.sb("borow", [1, 2048], BF16)
    bgv = p.sb("bgv", [1, 2048], BF16)
    q83 = p.sb("q83", [128, 8192], I8)
    xq3b = p.sb("xq3b", [128, 8192], BF16)
    g1e = p.sb("g1e", [128, 512], F32)
    sg1 = p.sb("sg1", [128, 512], F32)
    yt = [p.sb(f"yt{i}", [128, 512], F32) for i in range(2)]
    x1t = [p.sb(f"x1t{i}", [128, 2048], BF16) for i in range(2)]
    x1sl = [p.sb(f"x1sl{i}", [128, 512], BF16) for i in range(2)]
    xnf1 = p.sb("xnf1", [128, 2048], F32)
    xnb1 = [p.sb(f"xnb1_{i}", [128, 2048], BF16) for i in range(2)]
    q8a = p.sb("q8a", [128, 2048], I8)
    s1t = [p.sb(f"s1t{i}", [128, 512], BF16) for i in range(2)]
    gcol = p.sb("gcol", [128, 128], F32)
    scol = p.sb("scol", [128, 48], F32)
    ccol = p.sb("ccol", [128, 32], F32)
    crow = p.sb("crow", [1, 32], F32)
    onesf1 = p.sb("onesf1", [1, 128], F32)
    ones128f = p.sb("ones128f", [128, 1], F32)
    idbf = p.sb("s_idbf", [128, 128], BF16)
    idf32 = p.sb("s_idf32", [128, 128], F32)
    cpool = p.sb("cpool1", [128, 8], F32)
    lncol = p.sb("lncol1", [128, 16], F32)
    p.junkb = p.sb("junkb1", [128, 2048], BF16)

    PS = [p.ps(f"PT{b}", [128, 512], F32) for b in range(8)]
    PSbf = [ps.bitcast(BF16) for ps in PS]
    ln_eps = cpool[:, 0:1]

    p.dma("sync", idbf[:], io["idbf"][:], sem="init")
    tok_init = p.dma("sync", idf32[:], io["idf32"][:], sem="init")
    p.op("vector", "memset", cpool[:, 0:1], 1e-5)
    p.op("vector", "memset", onesf1[:], 1.0)
    t_ms = p.op("vector", "memset", ones128f[:], 1.0, done=True)

    p.dma("sync", ccol[:, 0:16], io["gsc"][:, 0:16], sem="init")
    tok_init = p.dma("sync", borow[:], io["wbias"][2:3, 0:2048], sem="init")
    tgs = tbo = tok_init
    tok_gb = [tgs, tgs, tgs]
    tok_bs = tbo
    tok_w8_done = tbo
    GAM = lambda mi: ccol[:, 8 + mi * 2: 9 + mi * 2]
    GR = lambda mi: ccol[:, 9 + mi * 2: 10 + mi * 2]

    tok_lnb = [None, None]
    tok_x1e = [None, None]
    tok_s1l = [None, None]
    tok_y = [None, None]
    tok_xqT_free = None
    tok_gv_free = tok_w8_done
    tok_wtb = [None, None]
    tok_bgv = [None, None]
    tok_q8a = None
    tok_xq3_free = None
    tok_first_p3 = ("PAIR", tok_w8_done, tok_bs)
    tmm2 = None
    tmm3 = None
    for g in range(NG):
        tg0 = g * TG
        t_xq = []
        t_xq_l = [None] * NTT
        scol_toks = [None] * NTT
        for tt in range(NTT):
            par = tt % 2
            row0 = tg0 + tt * 128
            tl = p.dma("sync", x1t[par][:], io["x1"][row0:row0 + 128, :],
                       after=[tok_lnb[par], tok_first_p3], sem=f"x1l{par}")
            _, tsq2, txn = emit_ln(p, None, xnf1[:], x1t[par][:], lncol,
                                   par * 8, ln_eps, [tl], want_f32=True)
            tok_lnb[par] = ("PAIR", txn, tsq2)
            trm = p.op("vector", "tensor_reduce", ccol[:, 16:17], xnf1[:],
                       AX.X, ALU.max, True, after=[txn], done=True)
            tcl = p.op("vector", "tensor_single_scalar", ccol[:, 17:18],
                       ccol[:, 16:17], 1e-5, ALU.max, after=[trm], done=True)
            trc = p.op("vector", "reciprocal", ccol[:, 18:19], ccol[:, 17:18],
                       after=[tcl], done=True)
            ts7 = p.op("vector", "tensor_single_scalar", ccol[:, 19:20],
                       ccol[:, 18:19], 127.0, ALU.mult, after=[trc], done=True)
            tf0 = p.op("vector", "tensor_scalar", ccol[:, 20:21],
                       ccol[:, 17:18], GAM(0), None, ALU.mult,
                       after=tmax(tcl, tok_gb[0]), done=True)
            tf0 = p.op("vector", "tensor_single_scalar", ccol[:, 22:23],
                       ccol[:, 20:21], 1.0 / 127.0, ALU.mult, after=[tf0],
                       done=True)
            tf1 = p.op("vector", "tensor_scalar", ccol[:, 21:22],
                       ccol[:, 17:18], GAM(1), None, ALU.mult,
                       after=tmax(tcl, tok_gb[1]), done=True)
            tf1 = p.op("vector", "tensor_single_scalar", ccol[:, 23:24],
                       ccol[:, 21:22], 1.0 / 127.0, ALU.mult, after=[tf1],
                       done=True)
            c0 = tt * 12
            tcp = p.op("vector", "tensor_copy", scol[:, c0:c0 + 1],
                       ccol[:, 19:20], after=[ts7], done=True)
            tf0c = p.op("vector", "tensor_copy", scol[:, c0 + 1:c0 + 2],
                        ccol[:, 22:23], after=[tf0], done=True)
            tf1c = p.op("vector", "tensor_copy", scol[:, c0 + 2:c0 + 3],
                        ccol[:, 23:24], after=[tf1], done=True)
            scol_toks[tt] = ("PAIR", tcp, tf0c, tf1c)
            t8 = p.op("vector", "tensor_scalar", q8a[:], xnf1[:],
                      ccol[:, 19:20], None, ALU.mult,
                      after=tmax(ts7, tok_q8a), done=True)
            tqb = p.op("vector", "tensor_copy", xnb1[par][:], q8a[:],
                       after=[t8], done=True)
            tok_q8a = tqb
            ttp = p.op("tensor", "transpose", PS[3][0:1, 0:128],
                       scol[:, c0:c0 + 1], idf32[:],
                       after=tmax(tcp, tok_init, p.bank[3], tok_xqT_free),
                       done=True)
            trw = p.op("vector", "tensor_copy",
                       xqrow[0:1, tt * 128:(tt + 1) * 128],
                       PS[3][0:1, 0:128], after=[ttp], done=True)
            p.bank[3] = trw
            t_xq.append(trw)
            t_xq_tt = trw
            for fc in range(16):
                bank = 2 if fc % 2 == 0 else 6
                tm = p.op("tensor", "transpose", PSbf[bank][:, 0:128],
                          xnb1[par][:, fc * 128:(fc + 1) * 128], idbf[:],
                          after=tmax(tqb, tok_init, p.bank[bank], tok_xqT_free),
                          done=True)
                tev = p.op("vector", "tensor_copy",
                           xqT[:, fc * 512 + tt * 128: fc * 512 + (tt + 1) * 128],
                           PSbf[bank][:, 0:128], after=[tm], done=True)
                p.bank[bank] = tev
                t_xq.append(tev)
            t_xq_l[tt] = ("PAIR", tev, t_xq_tt)
        tok_xq = t_xq[-1]

        tok_gv = None
        for hc in range(16):
            par = 0
            twl = p.dma("gpsimd", wtb[par].rearrange("p (c o) -> p c o", c=16),
                        io["wgi8"][:, hc * 512:(hc + 1) * 512]
                        .rearrange("(c p) o -> p c o", p=128),
                        after=[tok_wtb[par], tok_w8_done], sem=f"wt{par}")
            tbg_ = p.dma("sync", bgv[0:1, par * 1024: par * 1024 + 512],
                         io["wbias"][0:1, hc * 512:(hc + 1) * 512],
                         after=[tok_bgv[par], tok_bs], sem=f"bg{par}")
            for tt in range(NTT):
                bank = tt % 2
                for fc in range(16):
                    p.op("tensor", "matmul", PS[bank][:, 0:512],
                         xqT[:, fc * 512 + tt * 128: fc * 512 + (tt + 1) * 128],
                         wtb[par][:, fc * 512:(fc + 1) * 512],
                         start=(fc == 0), stop=False,
                         after=tmax(twl, t_xq_l[tt], p.bank[bank]) if fc == 0
                         else None)
                p.op("tensor", "matmul", PS[bank][:, 0:512],
                     xqrow[0:1, tt * 128:(tt + 1) * 128],
                     bgv[0:1, par * 1024: par * 1024 + 512],
                     start=False, stop=True,
                     after=tmax(tbg_, t_xq_l[tt]),
                     done=True)
                tmm = ("eT", p.cnt["eT"])
                tge = p.op("scalar", "activation",
                           gsb[:, tt * 512:(tt + 1) * 512],
                           PS[bank][:, 0:512], AF.Silu,
                           scale=scol[:, tt * 12 + 1:tt * 12 + 2],
                           after=tmax(tmm, tok_gv, scol_toks[tt]), done=True)
                p.bank[bank] = tge
            tok_wtb[0] = tmm
            tok_bgv[0] = tmm
            par = 1
            twl2 = p.dma("gpsimd", wtb[par].rearrange("p (c o) -> p c o", c=16),
                         io["wvi8"][:, hc * 512:(hc + 1) * 512]
                         .rearrange("(c p) o -> p c o", p=128),
                         after=[tok_wtb[par], tok_bgv[par]], sem=f"wt{par}")
            tbv_ = p.dma("sync", bgv[0:1, par * 1024 + 512: par * 1024 + 1024],
                         io["wbias"][1:2, hc * 512:(hc + 1) * 512],
                         after=[tok_bgv[par], tok_bs], sem=f"bv{par}")
            for tt in range(NTT):
                bank = tt % 2
                for fc in range(16):
                    p.op("tensor", "matmul", PS[bank][:, 0:512],
                         xqT[:, fc * 512 + tt * 128: fc * 512 + (tt + 1) * 128],
                         wtb[par][:, fc * 512:(fc + 1) * 512],
                         start=(fc == 0), stop=False,
                         after=tmax(twl2, p.bank[bank]) if fc == 0 else None)
                p.op("tensor", "matmul", PS[bank][:, 0:512],
                     xqrow[0:1, tt * 128:(tt + 1) * 128],
                     bgv[0:1, par * 1024 + 512: par * 1024 + 1024],
                     start=False, stop=True,
                     after=tmax(tbv_) if tt == 0 else None,
                     done=True)
                tmm2 = ("eT", p.cnt["eT"])
                tve = p.op("scalar", "activation",
                           vsb2[:, tt * 512:(tt + 1) * 512],
                           PS[bank][:, 0:512], AF.Copy,
                           after=tmax(tmm2, tok_gv), done=True)
                p.bank[bank] = tve
                tok_gv = p.op("vector", "tensor_tensor",
                              gvsb[:, tt * 8192 + hc * 512:
                                   tt * 8192 + (hc + 1) * 512],
                              gsb[:, tt * 512:(tt + 1) * 512],
                              vsb2[:, tt * 512:(tt + 1) * 512], ALU.mult,
                              after=tmax(tge, tve, tok_gv_free), done=True)
            tok_wtb[par] = tmm2
            tok_bgv[par] = tmm2
        tok_gv_done = tok_gv

        t_rq = [None] * NTT
        trw_l = [None] * NTT
        tsl3_l = [None] * NTT
        for tt in range(NTT):
            c0 = tt * 12
            trm = p.op("vector", "tensor_reduce", ccol[:, 16:17],
                       gvsb[:, tt * 8192:(tt + 1) * 8192], AX.X, ALU.max, True,
                       after=[tok_gv_done], done=True)
            tgg = p.op("vector", "tensor_scalar", ccol[:, 17:18],
                       ccol[:, 16:17], scol[:, c0 + 2:c0 + 3], None, ALU.mult,
                       after=tmax(trm, scol_toks[tt]), done=True)
            tgc = p.op("vector", "tensor_single_scalar", ccol[:, 18:19],
                       ccol[:, 17:18], 1e-5, ALU.max, after=[tgg], done=True)
            trc = p.op("vector", "reciprocal", ccol[:, 19:20], ccol[:, 18:19],
                       after=[tgc], done=True)
            t1_ = p.op("vector", "tensor_single_scalar", ccol[:, 20:21],
                       ccol[:, 19:20], 127.0, ALU.mult, after=[trc], done=True)
            ts3 = p.op("vector", "tensor_scalar", ccol[:, 21:22],
                       ccol[:, 20:21], scol[:, c0 + 2:c0 + 3], None, ALU.mult,
                       after=[t1_], done=True)
            tg3 = p.op("vector", "tensor_scalar", ccol[:, 22:23],
                       ccol[:, 18:19], GAM(2), None, ALU.mult,
                       after=tmax(tgc, tok_gb[2]), done=True)
            tg3 = p.op("vector", "tensor_single_scalar", ccol[:, 23:24],
                       ccol[:, 22:23], 1.0 / 127.0, ALU.mult, after=[tg3],
                       done=True)
            tsl3 = p.op("vector", "tensor_copy", scol[:, c0 + 4:c0 + 5],
                        ccol[:, 23:24], after=[tg3], done=True)
            tsl3_l[tt] = tsl3
            tcp = p.op("vector", "tensor_copy", scol[:, c0 + 3:c0 + 4],
                       ccol[:, 20:21], after=[t1_], done=True)
            ttp = p.op("tensor", "transpose", PS[3][0:1, 0:128],
                       scol[:, c0 + 3:c0 + 4], idf32[:],
                       after=tmax(tcp, p.bank[3]), done=True)
            trw = p.op("vector", "tensor_copy",
                       r3row[0:1, tt * 128:(tt + 1) * 128],
                       PS[3][0:1, 0:128], after=[ttp], done=True)
            p.bank[3] = trw
            trw_l[tt] = trw
            t8 = p.op("vector", "tensor_scalar", q83[:],
                      gvsb[:, tt * 8192:(tt + 1) * 8192], ccol[:, 21:22], None,
                      ALU.mult,
                      after=tmax(ts3, tok_xq3_free,
                                 t_rq[tt - 1] if tt > 0 else None), done=True)
            tqb = p.op("vector", "tensor_copy", xq3b[:], q83[:], after=[t8],
                       done=True)
            for hcc in range(64):
                bank = 2 if hcc % 2 == 0 else 6
                tm = p.op("tensor", "transpose", PSbf[bank][:, 0:128],
                          xq3b[:, hcc * 128:(hcc + 1) * 128], idbf[:],
                          after=tmax(tqb, p.bank[bank]), done=True)
                tev = p.op("vector", "tensor_copy",
                           gvsb[:, tt * 8192 + hcc * 128:
                                tt * 8192 + (hcc + 1) * 128],
                           PSbf[bank][:, 0:128], after=[tm], done=True)
                p.bank[bank] = tev
            t_rq[tt] = tev

        OB = [0, 1, 4, 5]
        tok_lastmm = None
        tok_g1e_war = None
        for oc in range(4):
            for qb in range(4):
                parw = qb % 2
                twl = p.dma("gpsimd",
                            wtb[parw].rearrange("p (c o) -> p c o", c=16),
                            io["woi8"][qb * 2048:(qb + 1) * 2048,
                                       oc * 512:(oc + 1) * 512]
                            .rearrange("(c p) o -> p c o", p=128),
                            after=[tok_wtb[parw], tok_w8_done],
                            sem=f"wt{parw}")
                for tt in range(NTT):
                    bank = OB[tt]
                    for j in range(16):
                        hcc = qb * 16 + j
                        p.op("tensor", "matmul", PS[bank][:, 0:512],
                             gvsb[:, tt * 8192 + hcc * 128:
                                  tt * 8192 + (hcc + 1) * 128],
                             wtb[parw][:, j * 512:(j + 1) * 512],
                             start=(qb == 0 and j == 0), stop=False,
                             after=tmax(twl, t_rq[tt],
                                        p.bank[bank] if qb == 0 else None)
                             if j == 0 else None, done=(j == 15))
                tok_wtb[parw] = ("eT", p.cnt["eT"])
            for tt in range(NTT):
                bank = OB[tt]
                row0 = tg0 + tt * 128
                p.op("tensor", "matmul", PS[bank][:, 0:512],
                     r3row[0:1, tt * 128:(tt + 1) * 128],
                     borow[0:1, oc * 512:(oc + 1) * 512],
                     start=False, stop=True,
                     after=tmax(tbo, trw_l[tt]), done=True)
                tmm3 = ("eT", p.cnt["eT"])
                tok_lastmm = tmm3
                tsil = p.op("scalar", "activation", g1e[:], PS[bank][:, 0:512],
                            AF.Silu, scale=scol[:, tt * 12 + 4:tt * 12 + 5],
                            after=tmax(tmm3, tsl3_l[tt], tok_g1e_war),
                            done=True)
                p.bank[bank] = tsil
                pare = tt % 2
                tsl = p.dma("sync", s1t[pare][:],
                            io["s1"][row0:row0 + 128, oc * 512:(oc + 1) * 512],
                            after=[tok_s1l[pare]], sem=f"s1l{pare}")
                txl = p.dma("sync", x1sl[pare][:],
                            io["x1"][row0:row0 + 128, oc * 512:(oc + 1) * 512],
                            after=[tok_x1e[pare]], sem=f"x1e{pare}")
                tsg_ = p.op("vector", "tensor_tensor", sg1[:], s1t[pare][:],
                            g1e[:], ALU.mult, after=tmax(tsl, tsil), done=True)
                tok_s1l[pare] = tsg_
                tok_g1e_war = tsg_
                ty_ = p.op("vector", "tensor_tensor", yt[pare][:], sg1[:],
                           x1sl[pare][:], ALU.add,
                           after=tmax(tsg_, txl, tok_y[pare]), done=True)
                tok_x1e[pare] = ty_
                tok_y[pare] = p.dma(
                    "sync", io["y"][row0:row0 + 128, oc * 512:(oc + 1) * 512],
                    yt[pare][:], after=[ty_], sem=f"ys{pare}")
        tok_xq3_free = tok_lastmm
        tok_xqT_free = tmm2
        tok_gv_free = tok_lastmm

    p.wait("sync", [tok_y[0], tok_y[1]])
    return tok_y


import concourse.bass as _bass
from concourse.bass_utils import run_bass_kernel_spmd as _run_spmd

_CACHE = {}


def _build_fused_nc():
    nc = _bass.Bass("TRN2", target_bir_lowering=False)
    io = {}
    io["x0"] = nc.dram_tensor("x0", [2048, 2048], F32, kind="ExternalInput")
    io["qkvw_t"] = nc.dram_tensor("qkvw_t", [2048, 6144], F32, kind="ExternalInput")
    io["gatew_t"] = nc.dram_tensor("gatew_t", [2048, 2048], F32, kind="ExternalInput")
    io["outw_t"] = nc.dram_tensor("outw_t", [2048, 2048], F32, kind="ExternalInput")
    io["normw"] = nc.dram_tensor("normw", [1, 2048], F32, kind="ExternalInput")
    io["ffn0w"] = nc.dram_tensor("ffn0w", [2048, 2048], F32, kind="ExternalInput")
    io["ddt"] = nc.dram_tensor("d_ddt", [128, 16384], BF16, kind="ExternalInput")
    io["qdkd"] = nc.dram_tensor("d_qdkd", [128, 128], F32, kind="ExternalInput")
    io["idbf"] = nc.dram_tensor("d_idbf", [128, 128], BF16, kind="ExternalInput")
    io["idf32"] = nc.dram_tensor("d_idf32", [128, 128], F32, kind="ExternalInput")
    io["x1"] = nc.dram_tensor("x1", [2048, 2048], BF16, kind="Internal")
    io["wdeq_scr"] = nc.dram_tensor("wdeq_scr", [2048, 2048], BF16, kind="Internal")
    io["wq"] = nc.dram_tensor("wq", [2048, 2048], F32, kind="ExternalInput")
    io["wk"] = nc.dram_tensor("wk", [2048, 128], F32, kind="ExternalInput")
    io["wv"] = nc.dram_tensor("wv", [2048, 128], F32, kind="ExternalInput")
    for nm, shape, dt_ in [("pswap", [128, 128], BF16),
                           ("rcq", [128, 2048], BF16), ("rsq", [128, 2048], BF16),
                           ("rck", [128, 2048], BF16), ("rsk", [128, 2048], BF16),
                           ("tband", [128, 1024], BF16)]:
        io[nm] = nc.dram_tensor("d_" + nm, shape, dt_, kind="ExternalInput")
    io["s1"] = nc.dram_tensor("s1", [2048, 2048], BF16, kind="Internal")
    io["gwt"] = nc.dram_tensor("gwt", [2048, 8192], F32, kind="ExternalInput")
    io["vwt"] = nc.dram_tensor("vwt", [2048, 8192], F32, kind="ExternalInput")
    io["owt"] = nc.dram_tensor("owt", [8192, 2048], F32, kind="ExternalInput")
    io["gb"] = nc.dram_tensor("gb", [1, 8192], F32, kind="ExternalInput")
    io["vb"] = nc.dram_tensor("vb", [1, 8192], F32, kind="ExternalInput")
    io["ob"] = nc.dram_tensor("ob", [1, 2048], F32, kind="ExternalInput")
    io["wgi8"] = nc.dram_tensor("wgi8", [2048, 8192], I8, kind="Internal")
    io["wvi8"] = nc.dram_tensor("wvi8", [2048, 8192], I8, kind="Internal")
    io["woi8"] = nc.dram_tensor("woi8", [8192, 2048], I8, kind="Internal")
    io["wbias"] = nc.dram_tensor("wbias", [4, 8192], BF16, kind="Internal")
    io["gsc"] = nc.dram_tensor("gsc", [128, 16], F32, kind="Internal")
    io["y"] = nc.dram_tensor("y", [2048, 2048], F32, kind="ExternalOutput")

    es = contextlib.ExitStack()
    with es:
        p = Prog(nc, es)
        p.push_scope()
        p.prefix = "A_"
        build_l0(p, io)
        p.barrier()
        p.pop_scope()
        p.push_scope()
        p.prefix = "B_"
        build_l1a(p, io)
        p.barrier()
        p.pop_scope()
        p.push_scope()
        p.prefix = "C_"
        build_l1b(p, io)
        p.pop_scope()
        p.finalize()
    return nc


def kernel(L0_attn_hidden_states, L0_attn_qkv_w, L0_attn_out_w, L0_attn_gate_w,
           L0_attn_norm_w, L0_ffn_weight, L1_attn_Wq, L1_attn_Wk, L1_attn_Wv,
           L1_ffn_gate_w, L1_ffn_gate_b, L1_ffn_val_w, L1_ffn_val_b,
           L1_ffn_out_w, L1_ffn_out_b):
    C = host_consts()
    if "nc" not in _CACHE:
        _CACHE["nc"] = _build_fused_nc()

    x = np.ascontiguousarray(L0_attn_hidden_states.astype(np.float32, copy=False))
    shared = dict(
        qkvw_t=np.ascontiguousarray(L0_attn_qkv_w.T),
        gatew_t=np.ascontiguousarray(L0_attn_gate_w.T),
        outw_t=np.ascontiguousarray(L0_attn_out_w.T),
        normw=np.ascontiguousarray(L0_attn_norm_w.reshape(1, 2048)),
        ffn0w=np.ascontiguousarray(L0_ffn_weight),
        d_ddt=C["ddt"], d_qdkd=C["qdkd"], d_idbf=C["idbf"], d_idf32=C["idf32"],
        wq=np.ascontiguousarray(L1_attn_Wq), wk=np.ascontiguousarray(L1_attn_Wk),
        wv=np.ascontiguousarray(L1_attn_Wv),
        d_pswap=C["pswap"], d_rcq=C["rcq"], d_rsq=C["rsq"], d_rck=C["rck"],
        d_rsk=C["rsk"], d_tband=C["tband"],
        gwt=np.ascontiguousarray(L1_ffn_gate_w.T),
        vwt=np.ascontiguousarray(L1_ffn_val_w.T),
        owt=np.ascontiguousarray(L1_ffn_out_w.T),
        gb=np.ascontiguousarray(L1_ffn_gate_b.reshape(1, -1)),
        vb=np.ascontiguousarray(L1_ffn_val_b.reshape(1, -1)),
        ob=np.ascontiguousarray(L1_ffn_out_b.reshape(1, -1)))
    maps = [dict(shared, x0=np.ascontiguousarray(x[b])) for b in range(8)]
    res = _run_spmd(_CACHE["nc"], maps, core_ids=list(range(8)))
    return np.stack([np.asarray(r["y"], dtype=np.float32) for r in res.results])
